# version 17 (final)
# speedup vs baseline: 1.3794x; 1.0027x over previous
import contextlib
import math
import os
import numpy as np

import concourse.bass as bass
import concourse.mybir as mybir
from concourse.bass_utils import run_bass_kernel_spmd

F32 = mybir.dt.float32
BF16 = mybir.dt.bfloat16
I32 = mybir.dt.int32
U8 = mybir.dt.uint8
AF = mybir.ActivationFunctionType
ALU = mybir.AluOpType
AX = mybir.AxisListType

D = 1024
T = 2048
KT = 8
NCORES = 8

BLOCK_NAMES = ["l0_attn", "l0_ffn", "l1_attn", "l1_ffn",
               "l2_attn", "l2_ffn", "l3_attn", "l3_ffn"]


def _f(a):
    return np.ascontiguousarray(np.asarray(a, dtype=np.float32))


def host_prep(inp):
    hp = {}
    hp["c_ones_row"] = np.ones((1, 512), dtype=np.float32)
    hp["c_ones_col"] = np.ones((128, 1), dtype=np.float32)
    hp["c_m40"] = np.full((128, 1), -40.0, dtype=np.float32)
    hp["c_m15"] = np.full((128, 1), -15.0, dtype=np.float32)
    hp["c_warm"] = np.ones((128, 8), dtype=np.float32)
    hp["c_ident"] = np.eye(128, dtype=np.float32)
    inv = 1.0 / (10000.0 ** (np.arange(0, 64, 2, dtype=np.float64) / 64.0))
    fr = np.arange(T, dtype=np.float64)[None, :] * inv[:, None]
    hp["c_cos"] = _f(np.tile(np.cos(fr), (4, 1)))
    hp["c_sin"] = _f(np.tile(np.sin(fr), (4, 1)))

    qkv_w = np.asarray(inp["L0_attn_qkv_w"], np.float32)
    qkv_b = np.asarray(inp["L0_attn_qkv_b"], np.float32)
    heads = np.arange(16)
    pr = np.arange(32)
    even_rows = (heads[:, None] * 64 + 2 * pr[None, :]).reshape(-1)
    odd_rows = (heads[:, None] * 64 + 2 * pr[None, :] + 1).reshape(-1)
    for nm, base in (("q", 0), ("k", 1024)):
        hp[f"l0_w{nm}e"] = _f(qkv_w[base + even_rows, :].T)
        hp[f"l0_w{nm}o"] = _f(qkv_w[base + odd_rows, :].T)
        hp[f"l0_b{nm}e"] = _f(qkv_b[base + even_rows].reshape(1, 512))
        hp[f"l0_b{nm}o"] = _f(qkv_b[base + odd_rows].reshape(1, 512))
    hp["l0_wv"] = _f(qkv_w[2048:, :].T)
    hp["l0_bv"] = _f(qkv_b[2048:].reshape(1, 1024))
    pw = np.asarray(inp["L0_attn_proj_w"], np.float32)
    pw3 = pw.T.reshape(16, 64, 1024).transpose(1, 0, 2)
    hp["l0_wproj2"] = _f(np.concatenate([pw3[:, 0::2, :], pw3[:, 1::2, :]], axis=0))
    hp["l0_bprojr"] = _f(np.asarray(inp["L0_attn_proj_b"], np.float32).reshape(1, 1024))

    hp["l0f_gamma"] = _f(np.asarray(inp["L0_ffn_ln_gamma"]).reshape(8, 128).T)
    hp["l0f_beta"] = _f(np.asarray(inp["L0_ffn_ln_beta"]).reshape(8, 128).T)
    hp["l0f_w1q"] = np.ascontiguousarray(np.asarray(inp["L0_ffn_fc1_q"], np.int32))
    hp["l0f_w2q"] = np.ascontiguousarray(np.asarray(inp["L0_ffn_fc2_q"], np.int32))
    hp["l0f_s1"] = _f(inp["L0_ffn_fc1_scales"])
    hp["l0f_s2"] = _f(inp["L0_ffn_fc2_scales"])
    hp["l0f_b1"] = _f(np.asarray(inp["L0_ffn_fc1_bias"]).reshape(32, 128).T)
    hp["l0f_b2r"] = _f(np.asarray(inp["L0_ffn_fc2_bias"]).reshape(1, 1024))
    q_w = np.asarray(inp["L1_attn_q_w"], np.float32)
    k_w = np.asarray(inp["L1_attn_k_w"], np.float32)
    v_w = np.asarray(inp["L1_attn_v_w"], np.float32)
    hp["l1_wq"] = _f(q_w.reshape(8, 2, 64, 1024).transpose(3, 0, 1, 2).reshape(1024, 1024))
    hp["l1_wk"] = _f(k_w.reshape(8, 2, 64, 1024).transpose(3, 0, 1, 2).reshape(1024, 1024))
    hp["l1_wv"] = _f(v_w.transpose(2, 0, 1).reshape(1024, 1024))
    hp["l1_bq"] = _f(np.asarray(inp["L1_attn_q_b"]).reshape(1, 1024))
    hp["l1_bk"] = _f(np.asarray(inp["L1_attn_k_b"]).reshape(1, 1024))
    hp["l1_bv"] = _f(np.asarray(inp["L1_attn_v_b"]).reshape(1, 1024))
    lw = np.asarray(inp["L1_attn_l_w"], np.float32)
    lw3 = lw.T.reshape(16, 64, 1024).transpose(1, 0, 2)
    hp["l1_wl2"] = _f(np.concatenate([lw3[:, 0::2, :], lw3[:, 1::2, :]], axis=0))
    hp["l1_blr"] = _f(np.asarray(inp["L1_attn_l_b"]).reshape(1, 1024))
    hp["l1f_wr"] = _f(np.asarray(inp["L1_ffn_w_router"]).reshape(8, 128).T)
    hp["l1f_wb"] = _f(inp["L1_ffn_w_block"])

    hp["l2_wq"] = _f(inp["L2_attn_Wq"])
    hp["l2_wk"] = _f(inp["L2_attn_Wk"])
    hp["l2_wv"] = _f(inp["L2_attn_Wv"])
    hp["l2_w1"] = _f(inp["L2_attn_W1"])
    hp["l2_w2"] = _f(inp["L2_attn_W2"])
    hp["l2_gnw"] = _f(np.asarray(inp["L2_attn_gn_w"]).reshape(8, 128).T)
    hp["l2_gnb"] = _f(np.asarray(inp["L2_attn_gn_b"]).reshape(8, 128).T)
    lo, hi = np.log(1.0 / 512.0), np.log(1.0 / 32.0)
    gammas = 1.0 - np.exp(np.linspace(lo, hi, 8))
    tpos = np.arange(T, dtype=np.float64)
    qdec = np.stack([g ** tpos for g in gammas]).astype(np.float32)
    kdec = np.stack([g ** (-tpos) for g in gammas]).astype(np.float32)
    hp["l2_qdec"] = _f(np.repeat(qdec[:, None, :], 128, axis=1))
    hp["l2_kdec"] = _f(np.repeat(kdec[:, None, :], 128, axis=1))
    ml = np.arange(128)[:, None]
    tris = []
    for a in range(4):
        ql = np.arange(512)[None, :]
        tris.append((ql >= ml + a * 512 - 0) * 1.0 if False else
                    ((a * 512 // 512) * 0))
    tris = []
    for a in range(4):
        ql = np.arange(512)[None, :]
        tris.append((ql >= (128 * a + ml)).astype(np.float32))
    hp["l2_tri"] = _f(np.stack(tris))
    hp["c_eps"] = np.full((128, 1), 1e-5, dtype=np.float32)
    hp["c_ones_big"] = np.ones((128, T), dtype=np.float32)

    hp["l2f_wc"] = _f(np.asarray(inp["L2_ffn_Wc"]).reshape(1024, 128))
    hp["l2f_bc"] = _f(np.asarray(inp["L2_ffn_bc"]).reshape(128, 1))
    hp["l2f_tie"] = _f(np.tile(np.linspace(0.0, 1e-6, 32, dtype=np.float32)[None, :],
                               (128, 16)))
    hp["l2f_w1"] = _f(np.asarray(inp["L2_ffn_W1"]).transpose(2, 0, 1, 3).reshape(1024, 4096))
    hp["l2f_b1"] = _f(np.asarray(inp["L2_ffn_b1"]).reshape(32, 128).T)
    hp["l2f_w2"] = _f(np.asarray(inp["L2_ffn_W2"]).transpose(0, 1, 3, 2).reshape(4096, 1024))
    hp["l2f_b2r"] = _f(np.asarray(inp["L2_ffn_b2"]).reshape(128, 1024))
    esel = np.zeros((32, 128, 128), dtype=np.float32)
    for sc in range(32):
        for j in range(4):
            es_idx = sc * 4 + j
            esel[sc, es_idx, j * 32:(j + 1) * 32] = 0.0
    esel = np.zeros((32, 128, 128), dtype=np.float32)
    for sc in range(32):
        for j in range(4):
            esel[sc, 4 * sc + j, 32 * j:32 * (j + 1)] = 1.0
    hp["l2f_esel"] = _f(esel.reshape(32 * 128, 128))

    hp["l3_win"] = _f(np.asarray(inp["L3_attn_w_in"], np.float32).T)
    hp["l3_bin"] = _f(np.asarray(inp["L3_attn_b_in"]).reshape(8, 128).T)
    hp["l3_a"] = _f(np.asarray(inp["L3_attn_a"]).reshape(8, 128).T)
    hp["l3_b"] = _f(np.asarray(inp["L3_attn_b"]).reshape(8, 128).T)
    hp["l3_wout"] = _f(np.asarray(inp["L3_attn_w_out"], np.float32).T)
    hp["l3_boutr"] = _f(np.asarray(inp["L3_attn_b_out"]).reshape(1, 1024))

    hp["l3f_wg"] = _f(inp["L3_ffn_Wg"])
    hp["l3f_bgr"] = _f(np.asarray(inp["L3_ffn_bg"]).reshape(1, 8))
    hp["l3f_we"] = _f(np.asarray(inp["L3_ffn_We"], np.float32).transpose(0, 2, 1))
    hp["l3f_ber"] = _f(inp["L3_ffn_be"])
    return hp


HP_DTYPES = {"l0f_w1q": I32, "l0f_w2q": I32}


class Sem:
    def __init__(self, nc, es, name):
        self.h = es.enter_context(nc.semaphore(name))
        self.n = 0

    def inc(self, instr, k=1):
        instr.then_inc(self.h, k)
        self.n += k
        return instr

    def wait(self, eng, n=None):
        eng.wait_ge(self.h, self.n if n is None else n)


class KCtx:
    def __init__(self, nc, es, hp):
        self.nc = nc
        self.es = es
        self.dram = {}
        for name, arr in hp.items():
            dt = HP_DTYPES.get(name, F32)
            self.dram[name] = nc.dram_tensor(name, list(arr.shape), dt,
                                             kind="ExternalInput").ap()
        self.dram["x"] = nc.dram_tensor("x", [D, T], F32, kind="ExternalInput").ap()
        self.dram["out"] = nc.dram_tensor("out", [D, T], F32, kind="ExternalOutput").ap()
        if os.environ.get("L3F_DBG") == "1":
            for nm in ["dbg3_eall", "dbg3_scr", "dbg3_gcol"]:
                self.dram[nm] = nc.dram_tensor(nm, [128, 128], F32,
                                               kind="ExternalOutput").ap()
        if os.environ.get("L1F_DBG2") == "1":
            self.dram["dbgf_y"] = nc.dram_tensor("dbgf_y", [128, 8 * 512], F32,
                                                 kind="ExternalOutput").ap()
        if os.environ.get("L1F_DBG") == "1":
            self.dram["dbgf_mcol"] = nc.dram_tensor("dbgf_mcol", [128, 16], F32,
                                                    kind="ExternalOutput").ap()
            self.dram["dbgf_mrow"] = nc.dram_tensor("dbgf_mrow", [1, T], F32,
                                                    kind="ExternalOutput").ap()
            self.dram["dbgf_stok"] = nc.dram_tensor("dbgf_stok", [128, 16], F32,
                                                    kind="ExternalOutput").ap()
            self.dram["dbgf_srow"] = nc.dram_tensor("dbgf_srow", [1, T], F32,
                                                    kind="ExternalOutput").ap()
            self.dram["dbgf_sbig"] = nc.dram_tensor("dbgf_sbig", [128, T], F32,
                                                    kind="ExternalOutput").ap()
        if os.environ.get("L1_DBG") == "1":
            self.dram["dbg1_ctx"] = nc.dram_tensor("dbg1_ctx", [64, 8 * T], F32,
                                                   kind="ExternalOutput").ap()
            self.dram["dbg1_q"] = nc.dram_tensor("dbg1_q", [128, T], F32,
                                                 kind="ExternalOutput").ap()
            self.dram["dbg1_k"] = nc.dram_tensor("dbg1_k", [128, T], F32,
                                                 kind="ExternalOutput").ap()
            self.dram["dbg1_es"] = nc.dram_tensor("dbg1_es", [128, 2048], F32,
                                                  kind="ExternalOutput").ap()
            self.dram["dbg1_sums"] = nc.dram_tensor("dbg1_sums", [128, 64], F32,
                                                    kind="ExternalOutput").ap()
            self.dram["dbg1_vsb"] = nc.dram_tensor("dbg1_vsb", [128, 1024], F32,
                                                   kind="ExternalOutput").ap()
            self.dram["dbg1_rec"] = nc.dram_tensor("dbg1_rec", [128, 64], F32,
                                                   kind="ExternalOutput").ap()
            self.dram["dbg1_vscl"] = nc.dram_tensor("dbg1_vscl", [128, 128], F32,
                                                    kind="ExternalOutput").ap()
        if os.environ.get("L0_DBG") == "1":
            self.dram["dbg_ctx"] = nc.dram_tensor("dbg_ctx", [64, 8 * T], F32,
                                                  kind="ExternalOutput").ap()
            self.dram["dbg_den"] = nc.dram_tensor("dbg_den", [1, T], F32,
                                                  kind="ExternalOutput").ap()
            self.dram["dbg_rr"] = nc.dram_tensor("dbg_rr", [1, 2 * 1024], F32,
                                                 kind="ExternalOutput").ap()
            self.dram["dbg_qe"] = nc.dram_tensor("dbg_qe", [32, T], F32,
                                                 kind="ExternalOutput").ap()
            self.dram["dbg_es"] = nc.dram_tensor("dbg_es", [128, 2048], F32,
                                                 kind="ExternalOutput").ap()
            self.dram["dbg_ke"] = nc.dram_tensor("dbg_ke", [32, T], F32,
                                                 kind="ExternalOutput").ap()
        self.s_ld = Sem(nc, es, "s_ld")
        self.s_gld = Sem(nc, es, "s_gld")
        self.s_mm = Sem(nc, es, "s_mm")
        self.s_m2 = Sem(nc, es, "s_m2")
        self.s_v = Sem(nc, es, "s_v")
        self.s_a = Sem(nc, es, "s_a")
        self.s_st = Sem(nc, es, "s_st")
        self.xt = es.enter_context(nc.sbuf_tensor("xt", [128, KT * T], F32))
        self.xln = es.enter_context(nc.sbuf_tensor("xln", [128, KT * T], BF16))
        self.ones_row = es.enter_context(nc.sbuf_tensor("ones_row", [1, 512], BF16))
        self.ones_rowf = es.enter_context(nc.sbuf_tensor("ones_rowf", [1, 512], F32))
        self.ones_col = es.enter_context(nc.sbuf_tensor("ones_col", [128, 1], BF16))
        self.ones_colf = es.enter_context(nc.sbuf_tensor("ones_colf", [128, 1], F32))
        self.m40 = es.enter_context(nc.sbuf_tensor("m40col", [128, 1], F32))
        self.m15 = es.enter_context(nc.sbuf_tensor("m15col", [128, 1], F32))
        self.warm8 = es.enter_context(nc.sbuf_tensor("warm8", [128, 8], F32))
        self.epscol = es.enter_context(nc.sbuf_tensor("epscol", [128, 1], F32))
        self.ones64 = es.enter_context(nc.sbuf_tensor("ones64", [65, 512], BF16))
        self.identf = es.enter_context(nc.sbuf_tensor("identf", [128, 128], F32))

    _uid = 0

    def uname(self, base):
        KCtx._uid += 1
        return f"{base}_{KCtx._uid}"

    def block(self):
        return self.nc.Block()

    def load_consts(self):
        C = self
        with C.block() as blk:
            @blk.sync
            def _(sync):
                C.s_ld.inc(sync.dma_start(C.ones_rowf[:], C.dram["c_ones_row"]), 16)
                C.s_ld.inc(sync.dma_start(C.ones_colf[:], C.dram["c_ones_col"]), 16)
                C.s_ld.inc(sync.dma_start(C.identf[:], C.dram["c_ident"]), 16)
                C.s_ld.inc(sync.dma_start(C.m40[:], C.dram["c_m40"]), 16)
                C.s_ld.inc(sync.dma_start(C.m15[:], C.dram["c_m15"]), 16)
                C.s_ld.inc(sync.dma_start(C.warm8[:], C.dram["c_warm"]), 16)
                C.s_ld.inc(sync.dma_start(C.epscol[:], C.dram["c_eps"]), 16)

            @blk.gpsimd
            def _(gp):
                C.s_gld.inc(gp.dma_start(C.ones_row[:], C.dram["c_ones_row"]), 16)
                C.s_gld.inc(gp.dma_start(C.ones_col[:], C.dram["c_ones_col"]), 16)
                C.s_gld.inc(gp.dma_start(C.ones64[64:65, :], C.dram["c_ones_row"]), 16)

            @blk.vector
            def _(v):
                C.s_ld.wait(v)
                v.wait_ge(C.s_gld.h, C.s_gld.n)
                v.reciprocal(C.warm8[:], C.warm8[:])

    def xtile(self, k, c0=0, c1=T):
        return self.xt[:, k * T + c0: k * T + c1]

    def xlntile(self, k, c0=0, c1=T):
        return self.xln[:, k * T + c0: k * T + c1]

    def load_w_kt(self, gp, dst, src, ktiles):
        self.s_gld.inc(gp.dma_start(
            dst.rearrange("p (k n) -> p k n", k=ktiles),
            src.rearrange("(k p) n -> k p n", p=128).transpose([1, 0, 2])), 16)


def emit_load_x(C):
    with C.block() as blk:
        @blk.sync
        def _(sync):
            for k in range(KT):
                C.s_ld.inc(sync.dma_start(
                    C.xtile(k), C.dram["x"][k * 128:(k + 1) * 128, :]), 16)

        @blk.vector
        def _(v):
            C.s_ld.wait(v)


def emit_store_x(C):
    with C.block() as blk:
        @blk.sync
        def _(sync):
            for k in range(KT):
                C.s_st.inc(sync.dma_start(
                    C.dram["out"][k * 128:(k + 1) * 128, :], C.xtile(k)), 16)
            C.s_st.wait(sync)



def emit_ln(C, add_to_x=False, gamma=None, beta=None, src_bf=False):
    nc = C.nc
    es = contextlib.ExitStack()
    with es:
        sq = es.enter_context(nc.sbuf_tensor(C.uname("ln_sq"), [128, KT * T], BF16))
        rows = es.enter_context(nc.sbuf_tensor(C.uname("ln_rows"), [1, 4 * T], F32))
        mu_r, ms_r = rows[:, 0:T], rows[:, T:2 * T]
        r_r, rho_r = rows[:, 2 * T:3 * T], rows[:, 3 * T:4 * T]
        tmp = es.enter_context(nc.sbuf_tensor(C.uname("ln_tmp"), [128, 512], F32))

        es1 = contextlib.ExitStack()
        sps = es1.enter_context(nc.psum_tensor(C.uname("ln_sps"), [1, 2 * T], F32))
        with C.block() as blk:
            base_v0 = C.s_v.n

            @blk.vector
            def _(v):
                if not src_bf:
                    for k in range(KT):
                        v.tensor_copy(C.xlntile(k), C.xtile(k))
                for k in range(KT):
                    C.s_v.inc(v.tensor_mul(sq[:, k * T:(k + 1) * T],
                                           C.xlntile(k), C.xlntile(k)), 1)

            base_v = base_v0 + KT
            base_mm = C.s_mm.n

            @blk.tensor
            def _(t):
                t.wait_ge(C.s_v.h, base_v0 + 1)
                for c in range(4):
                    for k in range(KT):
                        t.matmul(sps[0:1, c * 512:(c + 1) * 512], C.ones_col[:],
                                 C.xln[:, k * T + c * 512: k * T + (c + 1) * 512],
                                 start=(k == 0), stop=(k == KT - 1))
                t.wait_ge(C.s_v.h, base_v)
                for c in range(4):
                    for k in range(KT):
                        ins = t.matmul(sps[0:1, T + c * 512:T + (c + 1) * 512],
                                       C.ones_col[:],
                                       sq[:, k * T + c * 512: k * T + (c + 1) * 512],
                                       start=(k == 0), stop=(k == KT - 1))
                C.s_mm.inc(ins, 1)

            base_a = C.s_a.n

            @blk.vector
            def _(v):
                v.wait_ge(C.s_mm.h, base_mm + 1)
                v.tensor_scalar_mul(mu_r, sps[0:1, 0:T], 1.0 / D)
                v.tensor_scalar_mul(ms_r, sps[0:1, T:2 * T], 1.0 / D)
                v.tensor_mul(r_r, mu_r, mu_r)
                v.tensor_sub(ms_r, ms_r, r_r)
                C.s_v.inc(v.tensor_scalar_add(ms_r, ms_r, 1e-5), 1)
                v.wait_ge(C.s_a.h, base_a + 1)
                C.s_v.inc(v.tensor_mul(rho_r, mu_r, r_r), 1)

            @blk.scalar
            def _(s):
                s.wait_ge(C.s_v.h, base_v + 1)
                s.activation(sps[0:1, 0:T], ms_r, AF.Ln)
                s.activation(sps[0:1, T:T + 8], C.warm8[0:1, :], AF.Identity)
                s.activation(sps[0:1, T:T + 8], C.warm8[0:1, :], AF.Identity)
                C.s_a.inc(s.activation(r_r, sps[0:1, 0:T], AF.Exp, scale=-0.5), 1)
        es1.close()

        es2 = contextlib.ExitStack()
        bps = es2.enter_context(nc.psum_tensor(C.uname("ln_bps"), [128, 2 * 1024], F32))
        with C.block() as blk:
            base_mm = C.s_mm.n
            base_v = C.s_v.n

            @blk.tensor
            def _(t):
                for c in range(4):
                    buf = c % 2
                    sl = slice(c * 512, (c + 1) * 512)
                    if c >= 2:
                        t.wait_ge(C.s_v.h, base_v + c - 1)
                    t.matmul(bps[:, buf * 1024:buf * 1024 + 512],
                             C.ones_rowf[0:1, 0:128], rho_r[:, sl],
                             start=True, stop=True)
                    ins = t.matmul(bps[:, buf * 1024 + 512:buf * 1024 + 1024],
                                   C.ones_rowf[0:1, 0:128], r_r[:, sl],
                                   start=True, stop=True)
                    C.s_mm.inc(ins, 1)

            @blk.vector
            def _(v):
                for c in range(4):
                    buf = c % 2
                    v.wait_ge(C.s_mm.h, base_mm + c + 1)
                    brho = bps[:, buf * 1024:buf * 1024 + 512]
                    br = bps[:, buf * 1024 + 512:buf * 1024 + 1024]
                    for k in range(KT):
                        srck = C.xlntile(k, c * 512, (c + 1) * 512) if src_bf \
                            else C.xtile(k, c * 512, (c + 1) * 512)
                        v.tensor_mul(tmp[:], srck, br)
                        ins = v.tensor_sub(tmp[:], tmp[:], brho)
                        dst = C.xlntile(k, c * 512, (c + 1) * 512)
                        if gamma is not None:
                            ins = v.tensor_scalar(dst, tmp[:],
                                                  gamma[:, k:k + 1], beta[:, k:k + 1],
                                                  ALU.mult, ALU.add)
                        else:
                            ins = v.tensor_copy(dst, tmp[:])
                        if add_to_x:
                            xd = C.xtile(k, c * 512, (c + 1) * 512)
                            ins = v.tensor_add(xd, xd, tmp[:])
                    C.s_v.inc(ins, 1)
        es2.close()



def _l0_proj_pass(C, ctx2, p):
    nc = C.nc
    esp = contextlib.ExitStack()
    with esp:
        wp = esp.enter_context(nc.sbuf_tensor(C.uname("l0_wp"), [128, 4 * 1024], BF16))
        bpr = esp.enter_context(nc.sbuf_tensor(C.uname("l0_bpr"), [1, 1024], BF16))
        po = esp.enter_context(nc.psum_tensor(C.uname("l0_po"), [128, 2 * 512], F32))
        with C.block() as blk:
            @blk.gpsimd
            def _(gp):
                C.s_gld.inc(gp.dma_start(
                    wp[:].rearrange("p (h n) -> p h n", h=4),
                    C.dram["l0_wproj2"][:, 4 * p:4 * (p + 1), :]), 16)
                if p == 0:
                    C.s_gld.inc(gp.dma_start(bpr[:], C.dram["l0_bprojr"]), 16)

            base_mm = C.s_mm.n
            base_v = C.s_v.n

            @blk.tensor
            def _(t):
                t.wait_ge(C.s_gld.h, C.s_gld.n)
                t.wait_ge(C.s_ld.h, C.s_ld.n)
                i = 0
                for ot in range(KT):
                    for c in range(4):
                        buf = i % 2
                        if i >= 2:
                            t.wait_ge(C.s_v.h, base_v + i - 1)
                        psl = po[:, buf * 512:(buf + 1) * 512]
                        first = True
                        if p == 0:
                            t.matmul(psl, bpr[0:1, ot * 128:(ot + 1) * 128],
                                     C.ones_row[0:1, :], start=True, stop=False)
                            first = False
                        for j in range(4):
                            ins = t.matmul(
                                psl, wp[:, j * 1024 + ot * 128: j * 1024 + (ot + 1) * 128],
                                ctx2[:, j * T + c * 512: j * T + (c + 1) * 512],
                                start=(first and j == 0), stop=(j == 3))
                        C.s_mm.inc(ins, 1)
                        i += 1

            @blk.vector
            def _(v):
                i = 0
                for ot in range(KT):
                    for c in range(4):
                        buf = i % 2
                        v.wait_ge(C.s_mm.h, base_mm + i + 1)
                        psl = po[:, buf * 512:(buf + 1) * 512]
                        dst = C.xtile(ot, c * 512, (c + 1) * 512)
                        C.s_v.inc(v.tensor_add(dst, dst, psl), 1)
                        i += 1


def emit_l0_attn(C):
    import os
    TR = int(os.environ.get("L0_TRUNC", "99"))
    nc = C.nc
    emit_ln(C)
    if TR < 1:
        return
    es = contextlib.ExitStack()
    with es:
        ctx2 = es.enter_context(nc.sbuf_tensor(C.uname("l0_ctx2"), [128, 4 * T], BF16))
        ctxs = es.enter_context(nc.sbuf_tensor(C.uname("l0_ctxs"), [64, T], BF16))
        cos = es.enter_context(nc.sbuf_tensor(C.uname("l0_cos"), [128, T], BF16))
        sin = es.enter_context(nc.sbuf_tensor(C.uname("l0_sin"), [128, T], BF16))
        with C.block() as blk:
            @blk.gpsimd
            def _(gp):
                C.s_gld.inc(gp.dma_start(cos[:], C.dram["c_cos"]), 16)
                C.s_gld.inc(gp.dma_start(sin[:], C.dram["c_sin"]), 16)

        NG = 1 if TR < 6 else 4
        for g in range(NG):
            esg = contextlib.ExitStack()
            with esg:
                wg = esg.enter_context(nc.sbuf_tensor(C.uname("l0_wg"), [128, 4 * KT * 128], BF16))
                brg = esg.enter_context(nc.sbuf_tensor(C.uname("l0_brg"), [1, 768], BF16))
                bqe, bqo = brg[:, 0:128], brg[:, 128:256]
                bke, bko = brg[:, 256:384], brg[:, 384:512]
                bvr = brg[:, 512:768]
                wvg = esg.enter_context(nc.sbuf_tensor(C.uname("l0_wvg"), [128, KT * 256], BF16))
                qko = esg.enter_context(nc.sbuf_tensor(C.uname("l0_qko"), [128, 2 * T], BF16))
                kko = esg.enter_context(nc.sbuf_tensor(C.uname("l0_kko"), [128, 2 * T], BF16))
                vsbg = esg.enter_context(nc.sbuf_tensor(C.uname("l0_vsbg"), [128, 16 * 260], BF16))

                with C.block() as blk:
                    @blk.gpsimd
                    def _(gp):
                        for i, nm in enumerate(["qe", "qo", "ke", "ko"]):
                            src = C.dram[f"l0_w{nm}"][:, g * 128:(g + 1) * 128]
                            C.load_w_kt(gp, wg[:, i * KT * 128:(i + 1) * KT * 128],
                                        src, KT)
                        C.load_w_kt(gp, wvg[:],
                                    C.dram["l0_wv"][:, g * 256:(g + 1) * 256], KT)
                        for nm, dst in [("l0_bqe", bqe), ("l0_bqo", bqo),
                                        ("l0_bke", bke), ("l0_bko", bko)]:
                            C.s_gld.inc(gp.dma_start(
                                dst, C.dram[nm][:, g * 128:(g + 1) * 128]), 16)
                        C.s_gld.inc(gp.dma_start(
                            bvr, C.dram["l0_bv"][:, g * 256:(g + 1) * 256]), 16)

                def wgt(i, k):
                    return wg[:, (i * KT + k) * 128:(i * KT + k + 1) * 128]

                esq = contextlib.ExitStack()
                if TR < 2:
                    continue
                with esq:
                    pq = esq.enter_context(nc.psum_tensor(C.uname("l0_pq"), [128, 2 * T], F32))
                    tbf = esq.enter_context(nc.sbuf_tensor(C.uname("l0_tbf"), [128, T], BF16))
                    qkg = esq.enter_context(nc.sbuf_tensor(C.uname("l0_qkg"), [128, 4 * T], BF16))
                    qeg, qog = qkg[:, 0:T], qkg[:, T:2 * T]
                    keg, kog = qkg[:, 2 * T:3 * T], qkg[:, 3 * T:4 * T]
                    for pair in range(2):
                        brow_e = bqe if pair == 0 else bke
                        brow_o = bqo if pair == 0 else bko
                        dste = qeg if pair == 0 else keg
                        dsto = qog if pair == 0 else kog
                        with C.block() as blk:
                            base_mm = C.s_mm.n

                            @blk.tensor
                            def _(t):
                                t.wait_ge(C.s_gld.h, C.s_gld.n)
                                for i, brow in enumerate([brow_e, brow_o]):
                                    for c in range(4):
                                        sl = slice(i * T + c * 512, i * T + (c + 1) * 512)
                                        t.matmul(pq[:, sl],
                                                 brow[0:1, :],
                                                 C.ones_row[0:1, :], start=True, stop=False)
                                        for k in range(KT):
                                            ins = t.matmul(
                                                pq[:, sl], wgt(pair * 2 + i, k),
                                                C.xlntile(k, c * 512, (c + 1) * 512),
                                                start=False, stop=(k == KT - 1))
                                    C.s_mm.inc(ins, 1)

                            @blk.vector
                            def _(v):
                                v.wait_ge(C.s_mm.h, base_mm + 2)
                                e_f, o_f = pq[:, 0:T], pq[:, T:2 * T]
                                v.tensor_mul(dste, e_f, cos[:])
                                v.tensor_mul(tbf[:], o_f, sin[:])
                                v.tensor_sub(dste, dste, tbf[:])
                                v.tensor_mul(dsto, o_f, cos[:])
                                v.tensor_mul(tbf[:], e_f, sin[:])
                                v.tensor_add(dsto, dsto, tbf[:])

                    if TR < 3:
                        continue
                    with C.block() as blk:
                        base_mm = C.s_mm.n
                        base_v = C.s_v.n

                        @blk.gpsimd
                        def _(gp):
                            for jj in range(4):
                                r0 = 64 * (jj % 2)
                                c0 = (jj // 2) * T
                                for st, se, so in [(qko, qeg, qog), (kko, keg, kog)]:
                                    C.s_gld.inc(gp.dma_start(
                                        st[r0:r0 + 32, c0:c0 + T],
                                        se[32 * jj:32 * jj + 32, :]), 16)
                                    C.s_gld.inc(gp.dma_start(
                                        st[r0 + 32:r0 + 64, c0:c0 + T],
                                        so[32 * jj:32 * jj + 32, :]), 16)

                        @blk.vector
                        def _(v):
                            v.tensor_copy(
                                vsbg[:].rearrange("p (m h c) -> p (m h) c", h=4, c=65)[:, :, 64:65],
                                C.ones_col[:].unsqueeze(1).broadcast_to([128, 64, 1]))
                            for m in range(16):
                                buf = m % 2
                                v.wait_ge(C.s_mm.h, base_mm + m + 1)
                                dst = vsbg[:, m * 260:(m + 1) * 260].rearrange(
                                    "p (h c) -> p h c", c=65)[:, :, 0:64]
                                srcp = pq[:, buf * 512:buf * 512 + 256].rearrange(
                                    "p (h c) -> p h c", c=64)
                                C.s_v.inc(v.tensor_copy(dst, srcp), 1)

                        @blk.tensor
                        def _(t):
                            for m in range(16):
                                buf = m % 2
                                if m >= 2:
                                    t.wait_ge(C.s_v.h, base_v + m - 1)
                                psl = pq[:, buf * 512:buf * 512 + 256]
                                t.matmul(psl, C.ones_row[0:1, 0:128],
                                         bvr[0:1, :], start=True, stop=False)
                                for k in range(KT):
                                    ins = t.matmul(psl,
                                                   C.xlntile(k, m * 128, (m + 1) * 128),
                                                   wvg[:, k * 256:(k + 1) * 256],
                                                   start=False, stop=(k == KT - 1))
                                C.s_mm.inc(ins, 1)

                NJ = 0 if TR < 4 else (1 if TR < 5 else 4)
                for j in range(NJ):
                    h = g * 4 + j
                    esh = contextlib.ExitStack()
                    with esh:
                        expS = esh.enter_context(nc.sbuf_tensor(C.uname("l0_expS"), [128, 2 * 2048], BF16))
                        dln = esh.enter_context(nc.sbuf_tensor(C.uname("l0_dln"), [65, 512], F32))
                        rr64 = esh.enter_context(nc.sbuf_tensor(C.uname("l0_rr64"), [65, 2048], BF16))
                        psS = esh.enter_context(nc.psum_tensor(C.uname("l0_psS"), [128, 4 * 512], F32))
                        psc = esh.enter_context(nc.psum_tensor(C.uname("l0_psc"), [65, 4 * 512], F32))

                        r0 = 64 * (j % 2)
                        c0 = (j // 2) * T
                        qsl = qko[r0:r0 + 64, c0:c0 + T]
                        ksl = kko[r0:r0 + 64, c0:c0 + T]
                        tp = (r0, 0)

                        with C.block() as blk:
                            base_mm = C.s_mm.n
                            base_m2 = C.s_m2.n
                            base_a = C.s_a.n
                            base_v = C.s_v.n

                            def ctx_mms(t, cm):
                                t.wait_ge(C.s_a.h, base_a + 2 * cm + 2)
                                for qc in range(4):
                                    ins = t.matmul(
                                        psc[:, qc * 512:(qc + 1) * 512],
                                        vsbg[:, cm * 260 + j * 65: cm * 260 + (j + 1) * 65],
                                        expS[:, (cm % 2) * 2048 + qc * 512: (cm % 2) * 2048 + (qc + 1) * 512],
                                        start=(cm == 0), stop=(cm == 15))
                                C.s_m2.inc(ins, 1)

                            @blk.tensor
                            def _(t):
                                t.wait_ge(C.s_gld.h, C.s_gld.n)
                                for m in range(16):
                                    if m >= 1:
                                        t.wait_ge(C.s_a.h, base_a + 2 * m)
                                    for qc in range(4):
                                        ins = t.matmul(psS[:, qc * 512:(qc + 1) * 512],
                                                       ksl[:, m * 128:(m + 1) * 128],
                                                       qsl[:, qc * 512:(qc + 1) * 512],
                                                       start=True, stop=True,
                                                       tile_position=tp)
                                        if qc in (1, 3):
                                            C.s_mm.inc(ins, 1)
                                    if m >= 1:
                                        ctx_mms(t, m - 1)
                                ctx_mms(t, 15)
                                for qc in range(4):
                                    t.wait_ge(C.s_v.h, base_v + qc + 1)
                                    ins = t.matmul(psS[0:64, qc * 512:(qc + 1) * 512],
                                                   C.ones64[64:65, 0:64],
                                                   rr64[64:65, qc * 512:(qc + 1) * 512],
                                                   start=True, stop=True)
                                    C.s_m2.inc(ins, 1)

                            @blk.scalar
                            def _(s):
                                for m in range(16):
                                    if m >= 2:
                                        s.wait_ge(C.s_m2.h, base_m2 + m - 1)
                                    b0 = (m % 2) * 2048
                                    for hf in range(2):
                                        s.wait_ge(C.s_mm.h, base_mm + 2 * m + hf + 1)
                                        C.s_a.inc(s.activation(
                                            expS[:, b0 + hf * 1024:b0 + (hf + 1) * 1024],
                                            psS[:, hf * 1024:(hf + 1) * 1024],
                                            AF.Exp, scale=0.125, bias=C.m40[:]), 1)

                            hj = (h % 8) // 2
                            if h % 2 == 0:
                                dst0 = ctx2[0:64, hj * T:(hj + 1) * T]
                            else:
                                dst0 = ctxs[0:64, :]

                            @blk.vector
                            def _(v):
                                v.wait_ge(C.s_m2.h, base_m2 + 16)
                                if h % 2 == 1:
                                    v.wait_ge(C.s_ld.h, C.s_ld.n)
                                for qc in range(4):
                                    dst = dst0[:, qc * 512:(qc + 1) * 512]
                                    v.tensor_copy(dst, psc[0:64, qc * 512:(qc + 1) * 512])
                                    v.reciprocal(dln[64:65, :],
                                                 psc[64:65, qc * 512:(qc + 1) * 512])
                                    C.s_v.inc(v.tensor_copy(rr64[64:65, qc * 512:(qc + 1) * 512],
                                                            dln[64:65, :]), 1)
                                for qc in range(4):
                                    dst = dst0[:, qc * 512:(qc + 1) * 512]
                                    v.wait_ge(C.s_m2.h, base_m2 + 17 + qc)
                                    C.s_v.inc(v.tensor_mul(
                                        dst, dst, psS[0:64, qc * 512:(qc + 1) * 512]), 1)

                            if h % 2 == 1:
                                base_vh = C.s_v.n

                                @blk.gpsimd
                                def _(gp):
                                    gp.wait_ge(C.s_v.h, base_vh)
                                    C.s_ld.inc(gp.dma_start(
                                        ctx2[64:128, hj * T:(hj + 1) * T],
                                        ctxs[0:64, :]), 16)

            if g in (1, 3) and TR >= 7:
                _l0_proj_pass(C, ctx2, g // 2)


def emit_l0_ffn(C):
    nc = C.nc
    es0 = contextlib.ExitStack()
    with es0:
        gb = es0.enter_context(nc.sbuf_tensor("l0f_gb", [128, 16], F32))
        gamma, beta = gb[:, 0:8], gb[:, 8:16]
        with C.block() as blk:
            @blk.sync
            def _(sync):
                C.s_ld.inc(sync.dma_start(gamma, C.dram["l0f_gamma"]), 16)
                C.s_ld.inc(sync.dma_start(beta, C.dram["l0f_beta"]), 16)

        emit_ln(C, add_to_x=True)
        emit_ln(C, gamma=gamma, beta=beta, src_bf=True)

        es = contextlib.ExitStack()
        with es:
            w1f = es.enter_context(nc.sbuf_tensor(C.uname("l0f_w1f"), [128, 2 * 1024], F32))
            w1bf = es.enter_context(nc.sbuf_tensor(C.uname("l0f_w1bf"), [128, KT * 1024], BF16))
            w2bf = es.enter_context(nc.sbuf_tensor(C.uname("l0f_w2bf"), [128, KT * 1024], BF16))
            hbf = es.enter_context(nc.sbuf_tensor(C.uname("l0f_hbf"), [128, KT * T], BF16))
            srow = es.enter_context(nc.sbuf_tensor(C.uname("l0f_srow"), [1, 2 * 1024], F32))
            b1c = es.enter_context(nc.sbuf_tensor(C.uname("l0f_b1c"), [128, 32], F32))
            b2r = es.enter_context(nc.sbuf_tensor(C.uname("l0f_b2r"), [1, 1024], BF16))
            psq = es.enter_context(nc.psum_tensor(C.uname("l0f_psq"), [128, 2 * 1024], F32))
            psh = es.enter_context(nc.psum_tensor(C.uname("l0f_psh"), [128, 2 * 512], F32))
            psz = es.enter_context(nc.psum_tensor(C.uname("l0f_psz"), [128, 2 * 512], F32))

            with C.block() as blk:
                @blk.sync
                def _(sync):
                    C.s_ld.inc(sync.dma_start(b1c[:], C.dram["l0f_b1"]), 16)

                @blk.gpsimd
                def _(gp):
                    C.s_gld.inc(gp.dma_start(b2r[:], C.dram["l0f_b2r"]), 16)

            for fc in range(4):
                for wi in range(2):
                    with C.block() as blk:
                        base_gld = C.s_gld.n
                        base_ld = C.s_ld.n
                        base_mm = C.s_mm.n
                        base_v = C.s_v.n
                        if wi == 0:
                            wsrc = C.dram["l0f_w1q"]
                            ssrc = C.dram["l0f_s1"]
                        else:
                            wsrc = C.dram["l0f_w2q"]
                            ssrc = C.dram["l0f_s2"]

                        @blk.gpsimd
                        def _(gp):
                            for k in range(KT):
                                buf = k % 2
                                if k >= 2:
                                    gp.wait_ge(C.s_v.h, base_v + k - 1)
                                if wi == 0:
                                    sl = wsrc[k * 128:(k + 1) * 128,
                                              fc * 1024:(fc + 1) * 1024]
                                else:
                                    sl = wsrc[fc * 1024 + k * 128: fc * 1024 + (k + 1) * 128, :]
                                C.s_gld.inc(gp.dma_start(
                                    w1f[:, buf * 1024:(buf + 1) * 1024], sl), 16)

                        @blk.sync
                        def _(sync):
                            for k in range(KT):
                                buf = k % 2
                                if k >= 2:
                                    sync.wait_ge(C.s_v.h, base_v + k - 1)
                                if wi == 0:
                                    srw = ssrc[k:k + 1, fc * 1024:(fc + 1) * 1024]
                                else:
                                    srw = ssrc[fc * 8 + k: fc * 8 + k + 1, :]
                                C.s_ld.inc(sync.dma_start(
                                    srow[:, buf * 1024:(buf + 1) * 1024], srw), 16)

                        @blk.tensor
                        def _(t):
                            for k in range(KT):
                                buf = k % 2
                                t.wait_ge(C.s_ld.h, base_ld + 16 * (k + 1))
                                if k >= 2:
                                    t.wait_ge(C.s_v.h, base_v + k - 1)
                                for sub in range(2):
                                    ins = t.matmul(
                                        psq[:, buf * 1024 + sub * 512: buf * 1024 + (sub + 1) * 512],
                                        C.ones_rowf[0:1, 0:128],
                                        srow[0:1, buf * 1024 + sub * 512: buf * 1024 + (sub + 1) * 512],
                                        start=True, stop=True)
                                C.s_mm.inc(ins, 1)

                        @blk.vector
                        def _(v):
                            wdst = w1bf if wi == 0 else w2bf
                            for k in range(KT):
                                buf = k % 2
                                v.wait_ge(C.s_mm.h, base_mm + k + 1)
                                v.wait_ge(C.s_gld.h, base_gld + 16 * (k + 1))
                                C.s_v.inc(v.scalar_tensor_tensor(
                                    wdst[:, k * 1024:(k + 1) * 1024],
                                    w1f[:, buf * 1024:(buf + 1) * 1024], 128.0,
                                    psq[:, buf * 1024:(buf + 1) * 1024],
                                    ALU.subtract, ALU.mult), 1)

                with C.block() as blk:
                    base_mm = C.s_mm.n
                    base_a = C.s_a.n

                    @blk.tensor
                    def _(t):
                        i = 0
                        for ot in range(8):
                            for c in range(4):
                                buf = i % 2
                                if i >= 2:
                                    t.wait_ge(C.s_a.h, base_a + i - 1)
                                psl = psh[:, buf * 512:(buf + 1) * 512]
                                for k in range(KT):
                                    ins = t.matmul(
                                        psl, w1bf[:, k * 1024 + ot * 128: k * 1024 + (ot + 1) * 128],
                                        C.xlntile(k, c * 512, (c + 1) * 512),
                                        start=(k == 0), stop=(k == KT - 1))
                                C.s_mm.inc(ins, 1)
                                i += 1

                    @blk.scalar
                    def _(s):
                        i = 0
                        for ot in range(8):
                            for c in range(4):
                                buf = i % 2
                                s.wait_ge(C.s_mm.h, base_mm + i + 1)
                                C.s_a.inc(s.activation(
                                    hbf[:, ot * T + c * 512: ot * T + (c + 1) * 512],
                                    psh[:, buf * 512:(buf + 1) * 512],
                                    AF.Silu, bias=b1c[:, fc * 8 + ot: fc * 8 + ot + 1]), 1)
                                i += 1

                with C.block() as blk:
                    base_mm = C.s_mm.n
                    base_v = C.s_v.n

                    @blk.tensor
                    def _(t):
                        i = 0
                        for ot in range(KT):
                            for c in range(4):
                                buf = i % 2
                                if i >= 2:
                                    t.wait_ge(C.s_v.h, base_v + i - 1)
                                psl = psz[:, buf * 512:(buf + 1) * 512]
                                first = True
                                if fc == 0:
                                    t.matmul(psl, b2r[0:1, ot * 128:(ot + 1) * 128],
                                             C.ones_row[0:1, :], start=True, stop=False)
                                    first = False
                                for k in range(KT):
                                    ins = t.matmul(
                                        psl, w2bf[:, k * 1024 + ot * 128: k * 1024 + (ot + 1) * 128],
                                        hbf[:, k * T + c * 512: k * T + (c + 1) * 512],
                                        start=(first and k == 0), stop=(k == KT - 1))
                                C.s_mm.inc(ins, 1)
                                i += 1

                    @blk.vector
                    def _(v):
                        i = 0
                        for ot in range(KT):
                            for c in range(4):
                                buf = i % 2
                                v.wait_ge(C.s_mm.h, base_mm + i + 1)
                                psl = psz[:, buf * 512:(buf + 1) * 512]
                                dst = C.xtile(ot, c * 512, (c + 1) * 512)
                                C.s_v.inc(v.scalar_tensor_tensor(
                                    dst, psl, 0.5, dst, ALU.mult, ALU.add), 1)
                                i += 1



def emit_l1_attn(C):
    nc = C.nc
    emit_ln(C)
    es = contextlib.ExitStack()
    with es:
        ctx2 = es.enter_context(nc.sbuf_tensor(C.uname("l1_ctx2"), [128, 4 * T], BF16))
        ctxs = es.enter_context(nc.sbuf_tensor(C.uname("l1_ctxs"), [64, T], BF16))
        for g in range(4):
            esg = contextlib.ExitStack()
            with esg:
                wq = esg.enter_context(nc.sbuf_tensor(C.uname("l1_wq"), [128, 2 * KT * 128], BF16))
                wk = esg.enter_context(nc.sbuf_tensor(C.uname("l1_wk"), [128, 2 * KT * 128], BF16))
                qg = esg.enter_context(nc.sbuf_tensor(C.uname("l1_qg"), [128, 2 * T], BF16))
                kg = esg.enter_context(nc.sbuf_tensor(C.uname("l1_kg"), [128, 2 * T], BF16))
                brg = esg.enter_context(nc.sbuf_tensor(C.uname("l1_brg"), [1, 768], BF16))
                bq, bk = brg[:, 0:256], brg[:, 256:512]
                bv = brg[:, 512:768]

                wvg = esg.enter_context(nc.sbuf_tensor(C.uname("l1_wvg"), [128, KT * 256], BF16))
                vsbg = esg.enter_context(nc.sbuf_tensor(C.uname("l1_vsbg"), [128, 16 * 256], BF16))
                with C.block() as blk:
                    @blk.gpsimd
                    def _(gp):
                        C.load_w_kt(gp, wq[:], C.dram["l1_wq"][:, g * 256:(g + 1) * 256], KT)
                        C.load_w_kt(gp, wk[:], C.dram["l1_wk"][:, g * 256:(g + 1) * 256], KT)
                        C.load_w_kt(gp, wvg[:], C.dram["l1_wv"][:, g * 256:(g + 1) * 256], KT)
                        C.s_gld.inc(gp.dma_start(bq, C.dram["l1_bq"][:, g * 256:(g + 1) * 256]), 16)
                        C.s_gld.inc(gp.dma_start(bk, C.dram["l1_bk"][:, g * 256:(g + 1) * 256]), 16)
                        C.s_gld.inc(gp.dma_start(bv, C.dram["l1_bv"][:, g * 256:(g + 1) * 256]), 16)

                esq = contextlib.ExitStack()
                with esq:
                    pq = esq.enter_context(nc.psum_tensor(C.uname("l1_pq"), [128, 2 * T], F32))
                    with C.block() as blk:
                        base_mm = C.s_mm.n
                        base_v = C.s_v.n

                        @blk.tensor
                        def _(t):
                            t.wait_ge(C.s_gld.h, C.s_gld.n)
                            for pi, (wsb, brow) in enumerate([(wq, bq), (wk, bk)]):
                                if pi == 1:
                                    t.wait_ge(C.s_v.h, base_v + 1)
                                for pt in range(2):
                                    for c in range(4):
                                        sl = slice(pt * T + c * 512, pt * T + (c + 1) * 512)
                                        t.matmul(pq[:, sl], brow[0:1, pt * 128:(pt + 1) * 128],
                                                 C.ones_row[0:1, :], start=True, stop=False)
                                        for k in range(KT):
                                            ins = t.matmul(
                                                pq[:, sl],
                                                wsb[:, k * 256 + pt * 128: k * 256 + (pt + 1) * 128],
                                                C.xlntile(k, c * 512, (c + 1) * 512),
                                                start=False, stop=(k == KT - 1))
                                C.s_mm.inc(ins, 1)
                            t.wait_ge(C.s_v.h, base_v + 2)
                            for m in range(16):
                                buf = m % 2
                                if m >= 2:
                                    t.wait_ge(C.s_v.h, base_v + m + 1)
                                psl = pq[:, buf * 512:buf * 512 + 256]
                                t.matmul(psl, C.ones_row[0:1, 0:128],
                                         bv[0:1, :], start=True, stop=False)
                                for k in range(KT):
                                    ins = t.matmul(psl,
                                                   C.xlntile(k, m * 128, (m + 1) * 128),
                                                   wvg[:, k * 256:(k + 1) * 256],
                                                   start=False, stop=(k == KT - 1))
                                C.s_mm.inc(ins, 1)

                        @blk.vector
                        def _(v):
                            v.wait_ge(C.s_mm.h, base_mm + 1)
                            for cc in range(4):
                                ins = v.tensor_copy(qg[:, cc * 1024:(cc + 1) * 1024],
                                                    pq[:, cc * 1024:(cc + 1) * 1024])
                            C.s_v.inc(ins, 1)
                            v.wait_ge(C.s_mm.h, base_mm + 2)
                            for cc in range(4):
                                ins = v.tensor_copy(kg[:, cc * 1024:(cc + 1) * 1024],
                                                    pq[:, cc * 1024:(cc + 1) * 1024])
                            C.s_v.inc(ins, 1)
                            for m in range(16):
                                buf = m % 2
                                v.wait_ge(C.s_mm.h, base_mm + 3 + m)
                                C.s_v.inc(v.tensor_copy(
                                    vsbg[:, m * 256:(m + 1) * 256],
                                    pq[:, buf * 512:buf * 512 + 256]), 1)

                for j in range(4):
                    h = g * 4 + j
                    pt, hw = divmod(j, 2)
                    esh = contextlib.ExitStack()
                    with esh:
                        vscl = esh.enter_context(nc.sbuf_tensor(C.uname("l1_vscl"), [128, 16 * 64], BF16))
                        expS = esh.enter_context(nc.sbuf_tensor(C.uname("l1_expS"), [128, 8 * 2048], BF16))
                        sums = esh.enter_context(nc.sbuf_tensor(C.uname("l1_sums"), [128, 32], F32))
                        srec = esh.enter_context(nc.sbuf_tensor(C.uname("l1_srec"), [128, 16], F32))
                        spc = esh.enter_context(nc.sbuf_tensor(C.uname("l1_spc"), [128, 8], F32))
                        tots = esh.enter_context(nc.sbuf_tensor(C.uname("l1_tots"), [128, 16], F32))

                        qsl = qg[hw * 64:(hw + 1) * 64, pt * T:(pt + 1) * T]
                        ksl = kg[hw * 64:(hw + 1) * 64, pt * T:(pt + 1) * T]
                        tph = (64 * hw, 0)

                        esS = contextlib.ExitStack()
                        with esS:
                            psS = esS.enter_context(nc.psum_tensor(C.uname("l1_psS"), [128, 2 * 1024], F32))
                            psc = esS.enter_context(nc.psum_tensor(C.uname("l1_psc"), [64, T], F32))
                            with C.block() as blk:
                                base_mm = C.s_mm.n
                                base_a = C.s_a.n
                                base_v = C.s_v.n
                                base_m2 = C.s_m2.n

                                def ctx_mms(t, cm):
                                    t.wait_ge(C.s_v.h, base_v + cm + 1)
                                    for qc in range(2):
                                        for sub in range(2):
                                            o = qc * 1024 + sub * 512
                                            ins = t.matmul(
                                                psc[:, o:o + 512],
                                                vscl[:, cm * 64:(cm + 1) * 64],
                                                expS[:, (cm % 8) * 2048 + o: (cm % 8) * 2048 + o + 512],
                                                start=(cm == 0), stop=(cm == 15))
                                    C.s_m2.inc(ins, 1)

                                @blk.tensor
                                def _(t):
                                    for m in range(16):
                                        for qc in range(2):
                                            i = m * 2 + qc
                                            if i >= 2:
                                                t.wait_ge(C.s_a.h, base_a + i - 1)
                                            buf = i % 2
                                            for sub in range(2):
                                                q0 = qc * 1024 + sub * 512
                                                ins = t.matmul(
                                                    psS[:, buf * 1024 + sub * 512: buf * 1024 + (sub + 1) * 512],
                                                    ksl[:, m * 128:(m + 1) * 128],
                                                    qsl[:, q0:q0 + 512],
                                                    start=True, stop=True,
                                                    tile_position=tph)
                                            C.s_mm.inc(ins, 1)
                                        if m >= 8:
                                            ctx_mms(t, m - 8)
                                    for cm in range(8, 16):
                                        ctx_mms(t, cm)

                                @blk.scalar
                                def _(s):
                                    for i in range(32):
                                        m, qc = divmod(i, 2)
                                        s.wait_ge(C.s_mm.h, base_mm + i + 1)
                                        if m >= 8 and qc == 0:
                                            s.wait_ge(C.s_m2.h, base_m2 + m - 7)
                                        buf = i % 2
                                        C.s_a.inc(s.activation(
                                            expS[:, (m % 8) * 2048 + qc * 1024: (m % 8) * 2048 + (qc + 1) * 1024],
                                            psS[:, buf * 1024:(buf + 1) * 1024],
                                            AF.Exp, scale=0.125, bias=C.m15[:],
                                            accum_out=sums[:, i:i + 1]), 1)

                                @blk.vector
                                def _(v):
                                    for m in range(16):
                                        v.wait_ge(C.s_a.h, base_a + 2 * m + 2)
                                        v.tensor_add(tots[:, m:m + 1],
                                                     sums[:, 2 * m:2 * m + 1],
                                                     sums[:, 2 * m + 1:2 * m + 2])
                                        if m in (7, 15):
                                            lo = m - 7
                                            v.tensor_copy(spc[:], C.warm8[:])
                                            v.tensor_copy(spc[:], C.warm8[:])
                                            v.reciprocal(srec[:, lo:lo + 8],
                                                         tots[:, lo:lo + 8])
                                            v.tensor_copy(spc[:], C.warm8[:])
                                            v.tensor_copy(spc[:], C.warm8[:])
                                            for mm in range(lo, lo + 8):
                                                C.s_v.inc(v.tensor_scalar_mul(
                                                    vscl[:, mm * 64:(mm + 1) * 64],
                                                    vsbg[:, mm * 256 + j * 64: mm * 256 + (j + 1) * 64],
                                                    srec[:, mm:mm + 1]), 1)
                                    v.wait_ge(C.s_m2.h, base_m2 + 16)
                                    if h % 2 == 1:
                                        v.wait_ge(C.s_ld.h, C.s_ld.n)
                                        C.s_v.inc(v.tensor_copy(ctxs[0:64, :], psc[:, :]), 1)
                                    else:
                                        hj = (h % 8) // 2
                                        C.s_v.inc(v.tensor_copy(
                                            ctx2[0:64, hj * T:(hj + 1) * T], psc[:, :]), 1)

                                if h % 2 == 1:
                                    base_vh = C.s_v.n

                                    @blk.gpsimd
                                    def _(gp):
                                        gp.wait_ge(C.s_v.h, base_vh)
                                        C.s_ld.inc(gp.dma_start(
                                            ctx2[64:128, ((h % 8) // 2) * T:((h % 8) // 2 + 1) * T],
                                            ctxs[0:64, :]), 16)

            if g == 1 and os.environ.get("L1_DBG") == "1":
                with C.block() as blk:
                    @blk.gpsimd
                    def _(gp):
                        C.s_st.inc(gp.dma_start(C.dram["dbg1_ctx"], ctx64[0:64, 0:8 * T]), 16)
                        C.s_st.inc(gp.dma_start(C.dram["dbg1_q"], qg[:, 0:T]), 16)
                        C.s_st.inc(gp.dma_start(C.dram["dbg1_k"], kg[:, 0:T]), 16)
                        gp.wait_ge(C.s_st.h, C.s_st.n)
            if g in (1, 3):
                _l1_proj_pass(C, ctx2, g // 2)


def _l1_proj_pass(C, ctx2, p):
    nc = C.nc
    esp = contextlib.ExitStack()
    with esp:
        wp = esp.enter_context(nc.sbuf_tensor(C.uname("l1_wp"), [128, 4 * 1024], BF16))
        bpr = esp.enter_context(nc.sbuf_tensor(C.uname("l1_bpr"), [1, 1024], BF16))
        po = esp.enter_context(nc.psum_tensor(C.uname("l1_po"), [128, 2 * 512], F32))
        with C.block() as blk:
            @blk.gpsimd
            def _(gp):
                C.s_gld.inc(gp.dma_start(
                    wp[:].rearrange("p (h n) -> p h n", h=4),
                    C.dram["l1_wl2"][:, 4 * p:4 * (p + 1), :]), 16)
                if p == 0:
                    C.s_gld.inc(gp.dma_start(bpr[:], C.dram["l1_blr"]), 16)

            base_mm = C.s_mm.n
            base_v = C.s_v.n

            @blk.tensor
            def _(t):
                t.wait_ge(C.s_gld.h, C.s_gld.n)
                t.wait_ge(C.s_ld.h, C.s_ld.n)
                i = 0
                for ot in range(KT):
                    for c in range(4):
                        buf = i % 2
                        if i >= 2:
                            t.wait_ge(C.s_v.h, base_v + i - 1)
                        psl = po[:, buf * 512:(buf + 1) * 512]
                        first = True
                        if p == 0:
                            t.matmul(psl, bpr[0:1, ot * 128:(ot + 1) * 128],
                                     C.ones_row[0:1, :], start=True, stop=False)
                            first = False
                        for j in range(4):
                            ins = t.matmul(
                                psl, wp[:, j * 1024 + ot * 128: j * 1024 + (ot + 1) * 128],
                                ctx2[:, j * T + c * 512: j * T + (c + 1) * 512],
                                start=(first and j == 0), stop=(j == 3))
                        C.s_mm.inc(ins, 1)
                        i += 1

            @blk.vector
            def _(v):
                i = 0
                for ot in range(KT):
                    for c in range(4):
                        buf = i % 2
                        v.wait_ge(C.s_mm.h, base_mm + i + 1)
                        psl = po[:, buf * 512:(buf + 1) * 512]
                        dst = C.xtile(ot, c * 512, (c + 1) * 512)
                        C.s_v.inc(v.tensor_add(dst, dst, psl), 1)
                        i += 1



def emit_l1_ffn(C):
    nc = C.nc
    emit_ln(C)
    es = contextlib.ExitStack()
    with es:
        wrt = es.enter_context(nc.sbuf_tensor(C.uname("l1f_wrt"), [128, KT], BF16))
        srow = es.enter_context(nc.sbuf_tensor(C.uname("l1f_srow"), [1, T], F32))
        sbig = es.enter_context(nc.sbuf_tensor(C.uname("l1f_sbig"), [128, T], F32))
        stok = es.enter_context(nc.sbuf_tensor(C.uname("l1f_stok"), [128, 64], F32))
        mcol = es.enter_context(nc.sbuf_tensor(C.uname("l1f_mcol"), [128, 16], F32))
        mrow = es.enter_context(nc.sbuf_tensor(C.uname("l1f_mrow"), [1, T], BF16))
        wb = es.enter_context(nc.sbuf_tensor(C.uname("l1f_wb"), [128, KT * 1024], BF16))
        gsc = es.enter_context(nc.sbuf_tensor(C.uname("l1f_gsc"), [128, 9 * 512], F32))
        dbgy = es.enter_context(nc.sbuf_tensor(C.uname("l1f_dbgy"), [128, 8 * 512], F32)) \
            if os.environ.get("L1F_DBG2") == "1" else None
        psr = es.enter_context(nc.psum_tensor(C.uname("l1f_psr"), [1, 2048], F32))
        psm = es.enter_context(nc.psum_tensor(C.uname("l1f_psm"), [128, 512], F32))
        psh = es.enter_context(nc.psum_tensor(C.uname("l1f_psh"), [128, 2 * 512], F32))

        with C.block() as blk:
            @blk.gpsimd
            def _(gp):
                C.s_gld.inc(gp.dma_start(wrt[:], C.dram["l1f_wr"]), 16)
                C.load_w_kt(gp, wb[:], C.dram["l1f_wb"], KT)

        with C.block() as blk:
            base_mm = C.s_mm.n

            @blk.tensor
            def _(t):
                t.wait_ge(C.s_gld.h, C.s_gld.n)
                for c in range(4):
                    for k in range(KT):
                        ins = t.matmul(psr[0:1, c * 512:(c + 1) * 512],
                                       wrt[:, k:k + 1],
                                       C.xlntile(k, c * 512, (c + 1) * 512),
                                       start=(k == 0), stop=(k == KT - 1))
                C.s_mm.inc(ins, 1)
                for tt in range(16):
                    for k in range(KT):
                        ins = t.matmul(psm[:, tt:tt + 1],
                                       C.xlntile(k, tt * 128, (tt + 1) * 128),
                                       wrt[:, k:k + 1],
                                       start=(k == 0), stop=(k == KT - 1))
                C.s_mm.inc(ins, 1)

            @blk.vector
            def _(v):
                v.wait_ge(C.s_mm.h, base_mm + 1)
                v.tensor_copy(srow[0:1, :], psr[0:1, :])
                v.wait_ge(C.s_mm.h, base_mm + 2)
                C.s_v.inc(v.tensor_copy(stok[:, 0:16], psm[:, 0:16]), 1)

        with C.block() as blk:
            base_mm = C.s_mm.n
            base_v = C.s_v.n

            @blk.tensor
            def _(t):
                for c in range(4):
                    if c > 0:
                        t.wait_ge(C.s_v.h, base_v + c)
                    ins = t.matmul(psm[:, :], C.ones_rowf[0:1, 0:128],
                                   srow[:, c * 512:(c + 1) * 512], start=True, stop=True)
                    C.s_mm.inc(ins, 1)

            @blk.vector
            def _(v):
                for c in range(4):
                    v.wait_ge(C.s_mm.h, base_mm + c + 1)
                    C.s_v.inc(v.tensor_copy(sbig[:, c * 512:(c + 1) * 512], psm[:, :]), 1)
                def rank_of(tt):
                    b = (tt % 2) * 2048
                    for c in range(4):
                        v.tensor_scalar(gsc[:, b + c * 512:b + (c + 1) * 512],
                                        sbig[:, c * 512:(c + 1) * 512],
                                        stok[:, tt:tt + 1], None, ALU.is_gt)

                def red(tt):
                    b = (tt % 2) * 2048
                    v.reduce_sum(stok[:, 32 + tt:33 + tt], gsc[:, b:b + 2048], axis=AX.X)

                def cmp_(tt):
                    C.s_v.inc(v.tensor_scalar(mcol[:, tt:tt + 1], stok[:, 32 + tt:33 + tt],
                                              1536.0, None, ALU.is_lt), 1)

                for tt in range(16):
                    rank_of(tt)
                    if tt >= 1:
                        red(tt - 1)
                    if tt >= 2:
                        cmp_(tt - 2)
                red(15)
                cmp_(14)
                v.tensor_copy(gsc[:, 0:512], sbig[:, 0:512])
                cmp_(15)

        with C.block() as blk:
            base_mm = C.s_mm.n

            @blk.tensor
            def _(t):
                for tt in range(16):
                    ins = t.transpose(psr[0:1, tt * 128:(tt + 1) * 128],
                                      mcol[:, tt:tt + 1], C.identf[:])
                C.s_mm.inc(ins, 1)

            @blk.vector
            def _(v):
                C.s_mm.wait(v)
                C.s_v.inc(v.tensor_copy(mrow[0:1, :], psr[0:1, 0:T]), 1)

        if os.environ.get("L1F_DBG2") == "1":
            with C.block() as blk:
                @blk.gpsimd
                def _(gp):
                    C.s_st.inc(gp.dma_start(C.dram["dbgf_y"], dbgy[:]), 16)
                    gp.wait_ge(C.s_st.h, C.s_st.n)
        if os.environ.get("L1F_DBG") == "1":
            with C.block() as blk:
                @blk.gpsimd
                def _(gp):
                    C.s_st.inc(gp.dma_start(C.dram["dbgf_mcol"], mcol[:]), 16)
                    C.s_st.inc(gp.dma_start(C.dram["dbgf_mrow"], mrow[:]), 16)
                    C.s_st.inc(gp.dma_start(C.dram["dbgf_stok"], stok[:, 0:16]), 16)
                    C.s_st.inc(gp.dma_start(C.dram["dbgf_srow"], srow[:]), 16)
                    C.s_st.inc(gp.dma_start(C.dram["dbgf_sbig"], sbig[:]), 16)
                    gp.wait_ge(C.s_st.h, C.s_st.n)

        for c in range(4):
            with C.block() as blk:
                base_mm = C.s_mm.n
                base_a = C.s_a.n
                base_v = C.s_v.n
                base_m2 = C.s_m2.n
                hs = lambda i: gsc[:, (i % 2) * 512:(i % 2) * 512 + 512]
                w_ = lambda i: gsc[:, 1024 + (i % 2) * 512:1024 + (i % 2) * 512 + 512]
                hh = lambda i: gsc[:, 2048 + (i % 2) * 512:2048 + (i % 2) * 512 + 512]
                t4 = lambda i: gsc[:, 3072 + (i % 2) * 512:3072 + (i % 2) * 512 + 512]

                @blk.tensor
                def _(t):
                    ins = t.matmul(psm[:, :], C.ones_row[0:1, 0:128],
                                   mrow[0:1, c * 512:(c + 1) * 512], start=True, stop=True)
                    C.s_m2.inc(ins, 1)
                    for ot in range(KT):
                        buf = ot % 2
                        if ot >= 2:
                            t.wait_ge(C.s_v.h, base_v + 2 * (ot - 1))
                        psl = psh[:, buf * 512:(buf + 1) * 512]
                        for k in range(KT):
                            ins = t.matmul(psl, wb[:, k * 1024 + ot * 128: k * 1024 + (ot + 1) * 128],
                                           C.xlntile(k, c * 512, (c + 1) * 512),
                                           start=(k == 0), stop=(k == KT - 1))
                        C.s_mm.inc(ins, 1)

                @blk.scalar
                def _(s):
                    for ot in range(KT):
                        s.wait_ge(C.s_v.h, base_v + 2 * ot + 1)
                        C.s_a.inc(s.activation(t4(ot), w_(ot), AF.Tanh,
                                               scale=0.7978845608028654), 1)

                @blk.vector
                def _(v):
                    v.wait_ge(C.s_m2.h, base_m2 + 1)
                    for ot in range(KT + 1):
                        if ot < KT:
                            v.wait_ge(C.s_mm.h, base_mm + ot + 1)
                            psl = psh[:, (ot % 2) * 512:(ot % 2) * 512 + 512]
                            v.tensor_copy(hs(ot), psl)
                            v.tensor_mul(gsc[:, 4096:4608], hs(ot), hs(ot))
                            v.scalar_tensor_tensor(w_(ot), gsc[:, 4096:4608], 0.044715,
                                                   hs(ot), ALU.mult, ALU.mult)
                            v.tensor_add(w_(ot), w_(ot), hs(ot))
                            C.s_v.inc(v.tensor_scalar_mul(hh(ot), hs(ot), 0.5), 1)
                        if ot >= 1:
                            po = ot - 1
                            v.wait_ge(C.s_a.h, base_a + po + 1)
                            xlt = C.xlntile(po, c * 512, (c + 1) * 512)
                            xtt = C.xtile(po, c * 512, (c + 1) * 512)
                            tb = gsc[:, 4096:4608]
                            v.scalar_tensor_tensor(tb, t4(po), 1.0, hh(po), ALU.add, ALU.mult)
                            if c == 0 and os.environ.get("L1F_DBG2") == "1":
                                v.tensor_copy(dbgy[:, po * 512:(po + 1) * 512], tb)
                            v.tensor_sub(tb, tb, xlt)
                            v.tensor_mul(tb, tb, psm[:, :])
                            v.tensor_add(tb, tb, xlt)
                            C.s_v.inc(v.tensor_add(xtt, xtt, tb), 1)



def emit_l2_attn(C):
    nc = C.nc
    emit_ln(C)
    es = contextlib.ExitStack()
    with es:
        gat = es.enter_context(nc.sbuf_tensor(C.uname("l2_gat"), [128, KT * T], BF16))
        tri = es.enter_context(nc.sbuf_tensor(C.uname("l2_tri"), [128, 4 * 512], BF16))
        gnwb = es.enter_context(nc.sbuf_tensor(C.uname("l2_gnwb"), [128, 16], F32))
        gnw, gnb = gnwb[:, 0:8], gnwb[:, 8:16]
        rows = es.enter_context(nc.sbuf_tensor(C.uname("l2_rows"), [1, 4 * 512], F32))
        mu_r, e2_r = rows[:, 0:512], rows[:, 512:1024]
        var_r, rr_r = rows[:, 1024:1536], rows[:, 1536:2048]

        with C.block() as blk:
            @blk.gpsimd
            def _(gp):
                C.s_gld.inc(gp.dma_start(tri[:].rearrange("p (a n) -> p a n", a=4),
                                         C.dram["l2_tri"]), 16)

            @blk.sync
            def _(sync):
                C.s_ld.inc(sync.dma_start(gnw, C.dram["l2_gnw"]), 16)
                C.s_ld.inc(sync.dma_start(gnb, C.dram["l2_gnb"]), 16)

        esg = contextlib.ExitStack()
        with esg:
            w1 = esg.enter_context(nc.sbuf_tensor(C.uname("l2_w1"), [128, KT * 1024], BF16))
            psg = esg.enter_context(nc.psum_tensor(C.uname("l2_psg"), [128, 2 * 512], F32))
            with C.block() as blk:
                @blk.gpsimd
                def _(gp):
                    C.load_w_kt(gp, w1[:], C.dram["l2_w1"], KT)

                base_mm = C.s_mm.n
                base_a = C.s_a.n

                @blk.tensor
                def _(t):
                    t.wait_ge(C.s_gld.h, C.s_gld.n)
                    i = 0
                    for ot in range(KT):
                        for c in range(4):
                            buf = i % 2
                            if i >= 2:
                                t.wait_ge(C.s_a.h, base_a + i - 1)
                            psl = psg[:, buf * 512:(buf + 1) * 512]
                            for k in range(KT):
                                ins = t.matmul(psl, w1[:, k * 1024 + ot * 128: k * 1024 + (ot + 1) * 128],
                                               C.xlntile(k, c * 512, (c + 1) * 512),
                                               start=(k == 0), stop=(k == KT - 1))
                            C.s_mm.inc(ins, 1)
                            i += 1

                @blk.scalar
                def _(s):
                    i = 0
                    for ot in range(KT):
                        for c in range(4):
                            buf = i % 2
                            s.wait_ge(C.s_mm.h, base_mm + i + 1)
                            C.s_a.inc(s.activation(
                                gat[:, ot * T + c * 512: ot * T + (c + 1) * 512],
                                psg[:, buf * 512:(buf + 1) * 512], AF.Silu), 1)
                            i += 1

        for h in range(8):
            esh = contextlib.ExitStack()
            with esh:
                whd = esh.enter_context(nc.sbuf_tensor(C.uname("l2_whd"), [128, 3 * 128], BF16))
                dec = esh.enter_context(nc.sbuf_tensor(C.uname("l2_dec"), [128, 2 * T], F32))
                qdec, kdec = dec[:, 0:T], dec[:, T:2 * T]
                qb = esh.enter_context(nc.sbuf_tensor(C.uname("l2_qb"), [128, T], BF16))
                kb = esh.enter_context(nc.sbuf_tensor(C.uname("l2_kb"), [128, T], BF16))
                vsb = esh.enter_context(nc.sbuf_tensor(C.uname("l2_vsb"), [128, 16 * 128], BF16))
                sbf = esh.enter_context(nc.sbuf_tensor(C.uname("l2_sbf"), [128, 16 * 512], BF16))
                rsb = esh.enter_context(nc.sbuf_tensor(C.uname("l2_rsb"), [128, 2 * 512], BF16))
                rsq = esh.enter_context(nc.sbuf_tensor(C.uname("l2_rsq"), [128, 512], BF16))
                rtf = esh.enter_context(nc.sbuf_tensor(C.uname("l2_rtf"), [128, 512], F32))

                with C.block() as blk:
                    @blk.gpsimd
                    def _(gp):
                        for i, nm in enumerate(["l2_wq", "l2_wk", "l2_wv"]):
                            C.s_gld.inc(gp.dma_start(
                                whd[:, i * 128:(i + 1) * 128], C.dram[nm][h]), 16)
                        C.s_gld.inc(gp.dma_start(qdec, C.dram["l2_qdec"][h]), 16)
                        C.s_gld.inc(gp.dma_start(kdec, C.dram["l2_kdec"][h]), 16)

                esq = contextlib.ExitStack()
                with esq:
                    pq = esq.enter_context(nc.psum_tensor(C.uname("l2_pq"), [128, 2 * T], F32))
                    with C.block() as blk:
                        base_mm = C.s_mm.n
                        base_v = C.s_v.n

                        @blk.tensor
                        def _(t):
                            t.wait_ge(C.s_gld.h, C.s_gld.n)
                            for i in range(2):
                                for c in range(4):
                                    ins = t.matmul(pq[:, i * T + c * 512:i * T + (c + 1) * 512],
                                                   whd[:, i * 128:(i + 1) * 128],
                                                   C.xln[:, h * T + c * 512: h * T + (c + 1) * 512],
                                                   start=True, stop=True)
                                C.s_mm.inc(ins, 1)
                            t.wait_ge(C.s_v.h, base_v + 2)
                            for m in range(16):
                                buf = m % 2
                                if m >= 2:
                                    t.wait_ge(C.s_v.h, base_v + m + 1)
                                ins = t.matmul(pq[:, buf * 512:buf * 512 + 128],
                                               C.xln[:, h * T + m * 128: h * T + (m + 1) * 128],
                                               whd[:, 256:384], start=True, stop=True)
                                C.s_mm.inc(ins, 1)

                        @blk.vector
                        def _(v):
                            v.wait_ge(C.s_mm.h, base_mm + 1)
                            for c in range(4):
                                ins = v.tensor_mul(qb[:, c * 512:(c + 1) * 512],
                                                   pq[:, c * 512:(c + 1) * 512],
                                                   qdec[:, c * 512:(c + 1) * 512])
                            C.s_v.inc(ins, 1)
                            v.wait_ge(C.s_mm.h, base_mm + 2)
                            for c in range(4):
                                ins = v.tensor_mul(kb[:, c * 512:(c + 1) * 512],
                                                   pq[:, T + c * 512:T + (c + 1) * 512],
                                                   kdec[:, c * 512:(c + 1) * 512])
                            C.s_v.inc(ins, 1)
                            for m in range(16):
                                buf = m % 2
                                v.wait_ge(C.s_mm.h, base_mm + 3 + m)
                                C.s_v.inc(v.tensor_copy(vsb[:, m * 128:(m + 1) * 128],
                                                        pq[:, buf * 512:buf * 512 + 128]), 1)

                NMT = [4, 8, 12, 16]
                SM = [0, 4, 12, 24]
                esS = contextlib.ExitStack()
                with esS:
                    psS = esS.enter_context(nc.psum_tensor(C.uname("l2_psS"), [128, 2 * 512], F32))
                    psr = esS.enter_context(nc.psum_tensor(C.uname("l2_psr"), [128, 2 * 512], F32))
                    psrow = esS.enter_context(nc.psum_tensor(C.uname("l2_psrow"), [1, 1024], F32))
                    psB = esS.enter_context(nc.psum_tensor(C.uname("l2_psB"), [128, 2 * 512], F32))
                    psdv = psrow[0:1, 512:1024]
                    with C.block() as blk:
                        base_mm = C.s_mm.n
                        base_m2 = C.s_m2.n
                        base_v = C.s_v.n
                        base_a = C.s_a.n

                        vlane = []
                        vlane += [("mev", 0, m) for m in range(4)]
                        vlane += [("mev", 1, m) for m in range(4, 8)]
                        vlane += [("var", 0), ("rho", 0), ("gat", 0)]
                        vlane += [("mev", 2, m) for m in range(8, 12)]
                        vlane += [("var", 1), ("rho", 1), ("gat", 1)]
                        vlane += [("mev", 3, m) for m in range(12, 16)]
                        vlane += [("var", 2), ("rho", 2), ("gat", 2)]
                        vlane += [("var", 3), ("rho", 3), ("gat", 3)]
                        vpos = {op: i + 1 for i, op in enumerate(vlane)}

                        alane = []
                        alane += [("pev", 1, m) for m in range(4)]
                        alane += [("rsb", 0), ("rsq", 0), ("sqrt", 0)]
                        alane += [("pev", 2, m) for m in range(8)]
                        alane += [("rsb", 1), ("rsq", 1), ("sqrt", 1)]
                        alane += [("pev", 3, m) for m in range(12)]
                        alane += [("rsb", 2), ("rsq", 2), ("sqrt", 2)]
                        alane += [("rsb", 3), ("rsq", 3), ("sqrt", 3)]
                        apos = {op: i + 1 for i, op in enumerate(alane)}

                        _seq = ["ret0", "stats0", "ret1", "bcast0", "stats1",
                                "ret2", "bcast1", "stats2", "ret3", "bcast2",
                                "stats3", "bcast3"]
                        m2pos = {nm: i + 1 for i, nm in enumerate(_seq)}

                        def ev_wait(t, qc, m):
                            if m < 4 * qc:
                                t.wait_ge(C.s_a.h, base_a + apos[("pev", qc, m)])
                            else:
                                t.wait_ge(C.s_v.h, base_v + vpos[("mev", qc, m)])

                        @blk.tensor
                        def _(t):
                            def stiles(qc):
                                for m in range(NMT[qc]):
                                    buf = m % 2
                                    if m >= 2:
                                        ev_wait(t, qc, m - 2)
                                    elif qc >= 1:
                                        ev_wait(t, qc - 1, NMT[qc - 1] - 2 + m)
                                    ins = t.matmul(psS[:, buf * 512:(buf + 1) * 512],
                                                   kb[:, m * 128:(m + 1) * 128],
                                                   qb[:, qc * 512:(qc + 1) * 512],
                                                   start=True, stop=True)
                                    C.s_mm.inc(ins, 1)

                            def ret(qc):
                                if qc >= 2:
                                    t.wait_ge(C.s_a.h, base_a + apos[("rsb", qc - 2)])
                                for m in range(NMT[qc]):
                                    ev_wait(t, qc, m)
                                    ins = t.matmul(psr[:, (qc % 2) * 512:(qc % 2 + 1) * 512],
                                                   vsb[:, m * 128:(m + 1) * 128],
                                                   sbf[:, m * 512:(m + 1) * 512],
                                                   start=(m == 0), stop=(m == NMT[qc] - 1))
                                C.s_m2.inc(ins, 1)

                            def stats(qc):
                                t.wait_ge(C.s_a.h, base_a + apos[("rsq", qc)])
                                if qc >= 1:
                                    t.wait_ge(C.s_v.h, base_v + vpos[("rho", qc - 1)])
                                t.matmul(psrow[0:1, 0:512], C.ones_col[:], rsb[:, 0:512],
                                         start=True, stop=True)
                                ins = t.matmul(psrow[0:1, 512:1024], C.ones_col[:],
                                               rsq[:, 0:512], start=True, stop=True)
                                C.s_m2.inc(ins, 1)

                            def bcast(qc):
                                t.wait_ge(C.s_v.h, base_v + vpos[("rho", qc)])
                                if qc >= 1:
                                    t.wait_ge(C.s_v.h, base_v + vpos[("gat", qc - 1)])
                                t.matmul(psB[:, 0:512], C.ones_rowf[0:1, 0:128],
                                         rr_r, start=True, stop=True)
                                ins = t.matmul(psB[:, 512:1024], C.ones_rowf[0:1, 0:128],
                                               var_r, start=True, stop=True)
                                C.s_m2.inc(ins, 1)

                            stiles(0)
                            ret(0)
                            stiles(1)
                            stats(0)
                            ret(1)
                            bcast(0)
                            stiles(2)
                            stats(1)
                            ret(2)
                            bcast(1)
                            stiles(3)
                            stats(2)
                            ret(3)
                            bcast(2)
                            stats(3)
                            bcast(3)

                        @blk.vector
                        def _(v):
                            for op in vlane:
                                if op[0] == "mev":
                                    _, qc, m = op
                                    if m == 4 * qc and qc >= 1:
                                        v.wait_ge(C.s_m2.h, base_m2 + m2pos[f"ret{qc - 1}"])
                                    v.wait_ge(C.s_mm.h, base_mm + SM[qc] + m + 1)
                                    buf = m % 2
                                    a = m - 4 * qc
                                    C.s_v.inc(v.tensor_mul(
                                        sbf[:, m * 512:(m + 1) * 512],
                                        psS[:, buf * 512:(buf + 1) * 512],
                                        tri[:, a * 512:(a + 1) * 512]), 1)
                                elif op[0] == "var":
                                    qc = op[1]
                                    v.wait_ge(C.s_m2.h, base_m2 + m2pos[f"stats{qc}"])
                                    v.tensor_scalar_mul(mu_r, psrow[0:1, 0:512], 1.0 / 128.0)
                                    v.tensor_scalar_mul(e2_r, psrow[0:1, 512:1024], 1.0 / 128.0)
                                    v.tensor_mul(var_r, mu_r, mu_r)
                                    C.s_v.inc(v.tensor_sub(var_r, e2_r, var_r), 1)
                                elif op[0] == "rho":
                                    qc = op[1]
                                    v.wait_ge(C.s_a.h, base_a + apos[("sqrt", qc)])
                                    v.reciprocal(rr_r, psdv)
                                    C.s_v.inc(v.tensor_mul(var_r, mu_r, rr_r), 1)
                                elif op[0] == "gat":
                                    qc = op[1]
                                    v.wait_ge(C.s_m2.h, base_m2 + m2pos[f"bcast{qc}"])
                                    v.tensor_mul(rtf[:], rsb[:, 0:512], psB[:, 0:512])
                                    v.tensor_sub(rtf[:], rtf[:], psB[:, 512:1024])
                                    v.tensor_scalar(rtf[:], rtf[:], gnw[:, h:h + 1],
                                                    gnb[:, h:h + 1], ALU.mult, ALU.add)
                                    gsl = gat[:, h * T + qc * 512: h * T + (qc + 1) * 512]
                                    C.s_v.inc(v.tensor_add(gsl, gsl, rtf[:]), 1)

                        @blk.scalar
                        def _(s):
                            for op in alane:
                                if op[0] == "pev":
                                    _, qc, m = op
                                    if m == 0:
                                        s.wait_ge(C.s_m2.h, base_m2 + m2pos[f"ret{qc - 1}"])
                                    s.wait_ge(C.s_mm.h, base_mm + SM[qc] + m + 1)
                                    buf = m % 2
                                    C.s_a.inc(s.activation(
                                        sbf[:, m * 512:(m + 1) * 512],
                                        psS[:, buf * 512:(buf + 1) * 512],
                                        AF.Identity), 1)
                                elif op[0] == "rsb":
                                    qc = op[1]
                                    s.wait_ge(C.s_m2.h, base_m2 + m2pos[f"ret{qc}"])
                                    if qc >= 1:
                                        s.wait_ge(C.s_v.h, base_v + vpos[("gat", qc - 1)])
                                    C.s_a.inc(s.activation(
                                        rsb[:, 0:512],
                                        psr[:, (qc % 2) * 512:(qc % 2 + 1) * 512],
                                        AF.Identity), 1)
                                elif op[0] == "rsq":
                                    qc = op[1]
                                    C.s_a.inc(s.activation(
                                        rsq[:, 0:512],
                                        psr[:, (qc % 2) * 512:(qc % 2 + 1) * 512],
                                        AF.Square), 1)
                                elif op[0] == "sqrt":
                                    qc = op[1]
                                    s.wait_ge(C.s_v.h, base_v + vpos[("var", qc)])
                                    C.s_a.inc(s.activation(psdv, var_r, AF.Sqrt,
                                                           bias=C.epscol[0:1, :]), 1)

        esw = contextlib.ExitStack()
        with esw:
            w2 = esw.enter_context(nc.sbuf_tensor(C.uname("l2_w2"), [128, KT * 1024], BF16))
            pso = esw.enter_context(nc.psum_tensor(C.uname("l2_pso"), [128, 2 * 512], F32))
            with C.block() as blk:
                @blk.gpsimd
                def _(gp):
                    C.load_w_kt(gp, w2[:], C.dram["l2_w2"], KT)

                base_mm = C.s_mm.n
                base_v = C.s_v.n

                @blk.tensor
                def _(t):
                    t.wait_ge(C.s_gld.h, C.s_gld.n)
                    i = 0
                    for ot in range(KT):
                        for c in range(4):
                            buf = i % 2
                            if i >= 2:
                                t.wait_ge(C.s_v.h, base_v + i - 1)
                            psl = pso[:, buf * 512:(buf + 1) * 512]
                            for k in range(KT):
                                ins = t.matmul(psl, w2[:, k * 1024 + ot * 128: k * 1024 + (ot + 1) * 128],
                                               gat[:, k * T + c * 512: k * T + (c + 1) * 512],
                                               start=(k == 0), stop=(k == KT - 1))
                            C.s_mm.inc(ins, 1)
                            i += 1

                @blk.vector
                def _(v):
                    i = 0
                    for ot in range(KT):
                        for c in range(4):
                            buf = i % 2
                            v.wait_ge(C.s_mm.h, base_mm + i + 1)
                            dst = C.xtile(ot, c * 512, (c + 1) * 512)
                            C.s_v.inc(v.tensor_add(dst, dst, pso[:, buf * 512:(buf + 1) * 512]), 1)
                            i += 1



def emit_l2_ffn(C):
    nc = C.nc
    emit_ln(C)
    es = contextlib.ExitStack()
    with es:
        wc = es.enter_context(nc.sbuf_tensor(C.uname("l2f_wc"), [128, KT * 128], BF16))
        lg = es.enter_context(nc.sbuf_tensor(C.uname("l2f_lg"), [128, T], F32))
        gmax = es.enter_context(nc.sbuf_tensor(C.uname("l2f_gmax"), [128, 4 * 16], F32))
        perm = es.enter_context(nc.sbuf_tensor(C.uname("l2f_perm"), [128, T], BF16))
        tie = es.enter_context(nc.sbuf_tensor(C.uname("l2f_tie"), [128, 512], F32))
        bcc = es.enter_context(nc.sbuf_tensor(C.uname("l2f_bcc"), [128, 1], F32))
        b1c = es.enter_context(nc.sbuf_tensor(C.uname("l2f_b1c"), [128, 32], F32))
        b2w = es.enter_context(nc.sbuf_tensor(C.uname("l2f_b2w"), [128, 1024], BF16))
        esel = es.enter_context(nc.sbuf_tensor(C.uname("l2f_esel"), [128, 32 * 128], BF16))
        hsb = es.enter_context(nc.sbuf_tensor(C.uname("l2f_hsb"), [128, 32 * 512], BF16))
        w1c = es.enter_context(nc.sbuf_tensor(C.uname("l2f_w1c"), [128, 4 * KT * 128], BF16))
        w2b = es.enter_context(nc.sbuf_tensor(C.uname("l2f_w2b"), [128, 2 * 32 * 128], BF16))
        psl_ = es.enter_context(nc.psum_tensor(C.uname("l2f_psl"), [128, 4 * 512], F32))
        psm = es.enter_context(nc.psum_tensor(C.uname("l2f_psm"), [128, 4 * 512], F32))

        with C.block() as blk:
            @blk.gpsimd
            def _(gp):
                C.load_w_kt(gp, wc[:], C.dram["l2f_wc"], KT)
                C.load_w_kt(gp, esel[:], C.dram["l2f_esel"], 32)
                C.s_gld.inc(gp.dma_start(b2w[:], C.dram["l2f_b2r"]), 16)

            @blk.sync
            def _(sync):
                C.s_ld.inc(sync.dma_start(tie[:], C.dram["l2f_tie"]), 16)
                C.s_ld.inc(sync.dma_start(bcc[:], C.dram["l2f_bc"]), 16)
                C.s_ld.inc(sync.dma_start(b1c[:], C.dram["l2f_b1"]), 16)

        with C.block() as blk:
            base_mm = C.s_mm.n
            base_v = C.s_v.n

            @blk.tensor
            def _(t):
                t.wait_ge(C.s_gld.h, C.s_gld.n)
                t.wait_ge(C.s_ld.h, C.s_ld.n)
                for c in range(4):
                    buf = c % 2
                    if c >= 2:
                        t.wait_ge(C.s_v.h, base_v + c - 1)
                    for k in range(KT):
                        ins = t.matmul(psl_[:, buf * 512:(buf + 1) * 512],
                                       wc[:, k * 128:(k + 1) * 128],
                                       C.xlntile(k, c * 512, (c + 1) * 512),
                                       start=(k == 0), stop=(k == KT - 1))
                    C.s_mm.inc(ins, 1)

            @blk.vector
            def _(v):
                def stt(c):
                    buf = c % 2
                    v.wait_ge(C.s_mm.h, base_mm + c + 1)
                    C.s_v.inc(v.scalar_tensor_tensor(
                        lg[:, c * 512:(c + 1) * 512],
                        psl_[:, buf * 512:(buf + 1) * 512],
                        bcc[:, 0:1], tie[:], ALU.add, ALU.add), 1)

                def red(c):
                    v.reduce_max(gmax[:, c * 16:(c + 1) * 16].unsqueeze(2),
                                 lg[:, c * 512:(c + 1) * 512].rearrange(
                                     "p (g w) -> p g w", w=32), axis=AX.X)

                def isge(c):
                    v.tensor_tensor(
                        perm[:, c * 512:(c + 1) * 512].rearrange("p (g w) -> p g w", w=32),
                        lg[:, c * 512:(c + 1) * 512].rearrange("p (g w) -> p g w", w=32),
                        gmax[:, c * 16:(c + 1) * 16].unsqueeze(2).broadcast_to([128, 16, 32]),
                        ALU.is_ge)

                for c in range(6):
                    if c < 4:
                        stt(c)
                    if 1 <= c < 5:
                        red(c - 1)
                    if c >= 2:
                        isge(c - 2)

        for c in range(4):
            with C.block() as blk:
                base_gld = C.s_gld.n
                base_mm = C.s_mm.n
                base_a = C.s_a.n
                base_v = C.s_v.n
                base_m2 = C.s_m2.n

                @blk.gpsimd
                def _(gp):
                    for sc in range(32):
                        wbuf = sc % 4
                        if sc >= 4:
                            gp.wait_ge(C.s_mm.h, base_mm + sc - 3)
                        C.load_w_kt(gp, w1c[:, wbuf * KT * 128:(wbuf + 1) * KT * 128],
                                    C.dram["l2f_w1"][:, sc * 128:(sc + 1) * 128], KT)

                @blk.tensor
                def _(t):
                    for sc in range(32):
                        buf = sc % 4
                        wbuf = sc % 4
                        if sc >= 4:
                            t.wait_ge(C.s_v.h, base_v + sc - 3)
                        ins = t.matmul(psm[:, buf * 512:(buf + 1) * 512],
                                       esel[:, sc * 128:(sc + 1) * 128],
                                       perm[:, c * 512:(c + 1) * 512],
                                       start=True, stop=True)
                        C.s_m2.inc(ins, 1)
                        t.wait_ge(C.s_gld.h, base_gld + 16 * (sc + 1))
                        if sc >= 4:
                            t.wait_ge(C.s_a.h, base_a + sc - 3)
                        for k in range(KT):
                            ins = t.matmul(psl_[:, buf * 512:(buf + 1) * 512],
                                           w1c[:, (wbuf * KT + k) * 128:(wbuf * KT + k + 1) * 128],
                                           C.xlntile(k, c * 512, (c + 1) * 512),
                                           start=(k == 0), stop=(k == KT - 1))
                        C.s_mm.inc(ins, 1)

                @blk.scalar
                def _(s):
                    for sc in range(32):
                        buf = sc % 4
                        s.wait_ge(C.s_mm.h, base_mm + sc + 1)
                        C.s_a.inc(s.activation(
                            hsb[:, sc * 512:(sc + 1) * 512],
                            psl_[:, buf * 512:(buf + 1) * 512],
                            AF.Relu, bias=b1c[:, sc:sc + 1]), 1)

                @blk.vector
                def _(v):
                    for sc in range(32):
                        v.wait_ge(C.s_a.h, base_a + sc + 1)
                        v.wait_ge(C.s_m2.h, base_m2 + sc + 1)
                        C.s_v.inc(v.tensor_mul(
                            hsb[:, sc * 512:(sc + 1) * 512],
                            hsb[:, sc * 512:(sc + 1) * 512],
                            psm[:, (sc % 4) * 512:(sc % 4 + 1) * 512]), 1)

            with C.block() as blk:
                base_gld = C.s_gld.n
                base_mm = C.s_mm.n
                base_v = C.s_v.n

                @blk.gpsimd
                def _(gp):
                    for ot in range(KT):
                        wbuf = ot % 2
                        if ot >= 2:
                            gp.wait_ge(C.s_mm.h, base_mm + ot - 1)
                        C.load_w_kt(gp, w2b[:, wbuf * 32 * 128:(wbuf + 1) * 32 * 128],
                                    C.dram["l2f_w2"][:, ot * 128:(ot + 1) * 128], 32)

                @blk.tensor
                def _(t):
                    for ot in range(KT):
                        buf = ot % 2
                        wbuf = ot % 2
                        t.wait_ge(C.s_gld.h, base_gld + 16 * (ot + 1))
                        if ot >= 2:
                            t.wait_ge(C.s_v.h, base_v + ot - 1)
                        psl2 = psl_[:, buf * 512:(buf + 1) * 512]
                        t.matmul(psl2, b2w[:, ot * 128:(ot + 1) * 128],
                                 perm[:, c * 512:(c + 1) * 512], start=True, stop=False)
                        for sc in range(32):
                            ins = t.matmul(psl2,
                                           w2b[:, (wbuf * 32 + sc) * 128:(wbuf * 32 + sc + 1) * 128],
                                           hsb[:, sc * 512:(sc + 1) * 512],
                                           start=False, stop=(sc == 31))
                        C.s_mm.inc(ins, 1)

                @blk.vector
                def _(v):
                    for ot in range(KT):
                        buf = ot % 2
                        v.wait_ge(C.s_mm.h, base_mm + ot + 1)
                        dst = C.xtile(ot, c * 512, (c + 1) * 512)
                        C.s_v.inc(v.tensor_add(dst, dst, psl_[:, buf * 512:(buf + 1) * 512]), 1)



def emit_l3_attn(C):
    nc = C.nc
    emit_ln(C)
    es = contextlib.ExitStack()
    with es:
        hsb = es.enter_context(nc.sbuf_tensor(C.uname("l3_hsb"), [128, KT * T], F32))
        cols = es.enter_context(nc.sbuf_tensor(C.uname("l3_cols"), [128, 4 * KT], F32))
        bin_c, a_c = cols[:, 0:8], cols[:, 8:16]
        b_c, A_c = cols[:, 16:24], cols[:, 24:32]
        abig = es.enter_context(nc.sbuf_tensor(C.uname("l3_abig"), [128, 2 * T], F32))
        onesb = es.enter_context(nc.sbuf_tensor(C.uname("l3_onesb"), [128, T], F32))

        with C.block() as blk:
            @blk.sync
            def _(sync):
                C.s_ld.inc(sync.dma_start(bin_c, C.dram["l3_bin"]), 16)
                C.s_ld.inc(sync.dma_start(a_c, C.dram["l3_a"]), 16)
                C.s_ld.inc(sync.dma_start(b_c, C.dram["l3_b"]), 16)
                C.s_ld.inc(sync.dma_start(onesb[:], C.dram["c_ones_big"]), 16)

        esw = contextlib.ExitStack()
        with esw:
            wi = esw.enter_context(nc.sbuf_tensor(C.uname("l3_wi"), [128, KT * 1024], BF16))
            psh = esw.enter_context(nc.psum_tensor(C.uname("l3_psh"), [128, 2 * 512], F32))
            with C.block() as blk:
                @blk.gpsimd
                def _(gp):
                    C.load_w_kt(gp, wi[:], C.dram["l3_win"], KT)

                base_mm = C.s_mm.n
                base_a = C.s_a.n

                @blk.tensor
                def _(t):
                    t.wait_ge(C.s_gld.h, C.s_gld.n)
                    i = 0
                    for ot in range(KT):
                        for c in range(4):
                            buf = i % 2
                            if i >= 2:
                                t.wait_ge(C.s_a.h, base_a + i - 1)
                            psl = psh[:, buf * 512:(buf + 1) * 512]
                            for k in range(KT):
                                ins = t.matmul(psl, wi[:, k * 1024 + ot * 128: k * 1024 + (ot + 1) * 128],
                                               C.xlntile(k, c * 512, (c + 1) * 512),
                                               start=(k == 0), stop=(k == KT - 1))
                            C.s_mm.inc(ins, 1)
                            i += 1

                @blk.scalar
                def _(s):
                    C.s_a.inc(s.activation(A_c, a_c, AF.Sigmoid), 1)
                    i = 0
                    for ot in range(KT):
                        for c in range(4):
                            buf = i % 2
                            s.wait_ge(C.s_mm.h, base_mm + 1 + i)
                            C.s_a.inc(s.activation(
                                hsb[:, ot * T + c * 512: ot * T + (c + 1) * 512],
                                psh[:, buf * 512:(buf + 1) * 512],
                                AF.Identity, bias=bin_c[:, ot:ot + 1]), 1)
                            i += 1

        with C.block() as blk:
            @blk.vector
            def _(v):
                for k in range(KT):
                    v.tensor_scalar_mul(hsb[:, k * T:(k + 1) * T],
                                        hsb[:, k * T:(k + 1) * T], b_c[:, k:k + 1])
                v.tensor_scalar_mul(abig[:, 0:T], onesb[:], A_c[:, 0:1])
                v.tensor_scalar_mul(abig[:, T:2 * T], onesb[:], A_c[:, 1:2])
                for k in range(KT):
                    v.tensor_tensor_scan(C.xlntile(k), abig[:, (k % 2) * T:(k % 2) * T + T],
                                         hsb[:, k * T:(k + 1) * T],
                                         0.0, ALU.mult, ALU.add)
                    if k + 2 < KT:
                        v.tensor_scalar_mul(abig[:, (k % 2) * T:(k % 2) * T + T],
                                            onesb[:], A_c[:, k + 2:k + 3])

        eso = contextlib.ExitStack()
        with eso:
            wo = eso.enter_context(nc.sbuf_tensor(C.uname("l3_wo"), [128, KT * 1024], BF16))
            bor = eso.enter_context(nc.sbuf_tensor(C.uname("l3_bor"), [1, 1024], BF16))
            pso = eso.enter_context(nc.psum_tensor(C.uname("l3_pso"), [128, 2 * 512], F32))
            with C.block() as blk:
                @blk.gpsimd
                def _(gp):
                    C.load_w_kt(gp, wo[:], C.dram["l3_wout"], KT)
                    C.s_gld.inc(gp.dma_start(bor[:], C.dram["l3_boutr"]), 16)

                base_mm = C.s_mm.n
                base_v = C.s_v.n

                @blk.tensor
                def _(t):
                    t.wait_ge(C.s_gld.h, C.s_gld.n)
                    i = 0
                    for ot in range(KT):
                        for c in range(4):
                            buf = i % 2
                            if i >= 2:
                                t.wait_ge(C.s_v.h, base_v + i - 1)
                            psl = pso[:, buf * 512:(buf + 1) * 512]
                            t.matmul(psl, bor[0:1, ot * 128:(ot + 1) * 128],
                                     C.ones_row[0:1, :], start=True, stop=False)
                            for k in range(KT):
                                ins = t.matmul(psl, wo[:, k * 1024 + ot * 128: k * 1024 + (ot + 1) * 128],
                                               C.xlntile(k, c * 512, (c + 1) * 512),
                                               start=False, stop=(k == KT - 1))
                            C.s_mm.inc(ins, 1)
                            i += 1

                @blk.vector
                def _(v):
                    i = 0
                    for ot in range(KT):
                        for c in range(4):
                            buf = i % 2
                            v.wait_ge(C.s_mm.h, base_mm + i + 1)
                            dst = C.xtile(ot, c * 512, (c + 1) * 512)
                            C.s_v.inc(v.tensor_add(dst, dst, pso[:, buf * 512:(buf + 1) * 512]), 1)
                            i += 1



def emit_l3_ffn(C):
    nc = C.nc
    emit_ln(C)
    es = contextlib.ExitStack()
    with es:
        wg = es.enter_context(nc.sbuf_tensor(C.uname("l3f_wg"), [128, KT * 8], BF16))
        bgr = es.enter_context(nc.sbuf_tensor(C.uname("l3f_bgr"), [1, 8], BF16))
        eall = es.enter_context(nc.sbuf_tensor(C.uname("l3f_eall"), [128, 16 * 8], F32))
        scr = es.enter_context(nc.sbuf_tensor(C.uname("l3f_scr"), [128, 16 * 8], F32))
        mrows = es.enter_context(nc.sbuf_tensor(C.uname("l3f_mrows"), [128, 16 * 8], F32))
        gcol = es.enter_context(nc.sbuf_tensor(C.uname("l3f_gcol"), [128, 16 * 8], F32))
        esgates = contextlib.ExitStack()
        psg = esgates.enter_context(nc.psum_tensor(C.uname("l3f_psg"), [128, 2 * 512], F32))

        with C.block() as blk:
            @blk.gpsimd
            def _(gp):
                C.s_gld.inc(gp.dma_start(
                    wg[:].rearrange("p (k o) -> p k o", k=KT),
                    C.dram["l3f_wg"].rearrange("(k p) o -> k p o", p=128).transpose([1, 0, 2])), 16)
                C.s_gld.inc(gp.dma_start(bgr[:], C.dram["l3f_bgr"]), 16)

        with C.block() as blk:
            base_mm = C.s_mm.n
            base_a = C.s_a.n
            base_v = C.s_v.n

            @blk.tensor
            def _(t):
                t.wait_ge(C.s_gld.h, C.s_gld.n)
                for tt in range(16):
                    buf = tt % 2
                    if tt >= 2:
                        t.wait_ge(C.s_a.h, base_a + tt - 1)
                    psl = psg[:, buf * 512:buf * 512 + 8]
                    t.matmul(psl, C.ones_row[0:1, 0:128], bgr[0:1, :], start=True, stop=False)
                    for k in range(KT):
                        ins = t.matmul(psl, C.xlntile(k, tt * 128, (tt + 1) * 128),
                                       wg[:, k * 8:(k + 1) * 8],
                                       start=False, stop=(k == KT - 1))
                    C.s_mm.inc(ins, 1)

            @blk.vector
            def _(v):
                for tt in range(16):
                    buf = tt % 2
                    v.wait_ge(C.s_mm.h, base_mm + tt + 1)
                    C.s_v.inc(v.reduce_max(scr[:, tt * 8 + 1:tt * 8 + 2],
                                           psg[:, buf * 512:buf * 512 + 8], axis=AX.X,
                                           negate=True), 1)

            @blk.scalar
            def _(s):
                for tt in range(16):
                    buf = tt % 2
                    s.wait_ge(C.s_v.h, base_v + tt + 1)
                    C.s_a.inc(s.activation(eall[:, tt * 8:(tt + 1) * 8],
                                           psg[:, buf * 512:buf * 512 + 8],
                                           AF.Exp, bias=scr[:, tt * 8 + 1:tt * 8 + 2]), 1)

        with C.block() as blk:
            base_a = C.s_a.n
            base_v = C.s_v.n

            @blk.vector
            def _(v):
                C.s_a.wait(v)
                for tt in range(16):
                    v.reduce_max(scr[:, tt * 8 + 2:tt * 8 + 3],
                                 eall[:, tt * 8:(tt + 1) * 8], axis=AX.X)
                for tt in range(16):
                    v.tensor_scalar(mrows[:, tt * 8:(tt + 1) * 8],
                                    eall[:, tt * 8:(tt + 1) * 8],
                                    scr[:, tt * 8 + 2:tt * 8 + 3], None, ALU.is_ge)
                for tt in range(16):
                    v.scalar_tensor_tensor(gcol[:, tt * 8:(tt + 1) * 8],
                                           mrows[:, tt * 8:(tt + 1) * 8], -1.0,
                                           eall[:, tt * 8:(tt + 1) * 8],
                                           ALU.mult, ALU.mult)
                for tt in range(16):
                    v.tensor_add(gcol[:, tt * 8:(tt + 1) * 8],
                                 gcol[:, tt * 8:(tt + 1) * 8],
                                 eall[:, tt * 8:(tt + 1) * 8])
                for tt in range(16):
                    v.reduce_max(scr[:, tt * 8 + 3:tt * 8 + 4],
                                 gcol[:, tt * 8:(tt + 1) * 8], axis=AX.X)
                for tt in range(16):
                    v.tensor_scalar(gcol[:, tt * 8:(tt + 1) * 8],
                                    gcol[:, tt * 8:(tt + 1) * 8],
                                    scr[:, tt * 8 + 3:tt * 8 + 4], None, ALU.is_ge)
                for tt in range(16):
                    v.tensor_add(mrows[:, tt * 8:(tt + 1) * 8],
                                 mrows[:, tt * 8:(tt + 1) * 8],
                                 gcol[:, tt * 8:(tt + 1) * 8])
                for tt in range(16):
                    v.tensor_mul(gcol[:, tt * 8:(tt + 1) * 8],
                                 eall[:, tt * 8:(tt + 1) * 8],
                                 mrows[:, tt * 8:(tt + 1) * 8])
                for tt in range(16):
                    C.s_v.inc(v.reduce_sum(scr[:, tt * 8 + 4:tt * 8 + 5],
                                           gcol[:, tt * 8:(tt + 1) * 8], axis=AX.X), 1)

            @blk.scalar
            def _(s):
                s.wait_ge(C.s_v.h, C.s_v.n)
                for tt in range(16):
                    s.activation(scr[:, tt * 8 + 5:tt * 8 + 6],
                                 scr[:, tt * 8 + 4:tt * 8 + 5], AF.Ln)
                for tt in range(16):
                    ins = s.activation(scr[:, tt * 8 + 6:tt * 8 + 7],
                                       scr[:, tt * 8 + 5:tt * 8 + 6], AF.Exp, scale=-1.0)
                C.s_a.inc(ins, 1)

            @blk.vector
            def _(v):
                C.s_a.wait(v)
                for tt in range(16):
                    ins = v.tensor_scalar_mul(gcol[:, tt * 8:(tt + 1) * 8],
                                              gcol[:, tt * 8:(tt + 1) * 8],
                                              scr[:, tt * 8 + 6:tt * 8 + 7])
                C.s_v.inc(ins, 1)

        if os.environ.get("L3F_DBG") == "1":
            with C.block() as blk:
                @blk.gpsimd
                def _(gp):
                    C.s_st.inc(gp.dma_start(C.dram["dbg3_eall"], eall[:]), 16)
                    C.s_st.inc(gp.dma_start(C.dram["dbg3_scr"], scr[:]), 16)
                    C.s_st.inc(gp.dma_start(C.dram["dbg3_gcol"], gcol[:]), 16)
                    gp.wait_ge(C.s_st.h, C.s_st.n)
        esgates.close()

        esw2 = es.enter_context(nc.sbuf_tensor(C.uname("l3f_we2"), [128, 2 * KT * 1024], BF16))
        ber2 = es.enter_context(nc.sbuf_tensor(C.uname("l3f_ber2"), [1, 2 * 1024], BF16))
        base_gld0 = C.s_gld.n
        base_mm_ex = [0] * 9
        with C.block() as blk:
            @blk.gpsimd
            def _(gp):
                C.load_w_kt(gp, esw2[:, 0:KT * 1024], C.dram["l3f_we"][0], KT)
                C.s_gld.inc(gp.dma_start(ber2[0:1, 0:1024],
                                         C.dram["l3f_ber"][0:1, :]), 16)
        for e in range(8):
            esp = contextlib.ExitStack()
            with esp:
                we = esw2[:, (e % 2) * KT * 1024:(e % 2 + 1) * KT * 1024]
                ber = ber2[:, (e % 2) * 1024:(e % 2 + 1) * 1024]
                growe = esp.enter_context(nc.sbuf_tensor(C.uname("l3f_growe"), [1, T], BF16))
                gx = esp.enter_context(nc.sbuf_tensor(C.uname("l3f_gx"), [128, 2 * KT * 512], BF16))
                pst = esp.enter_context(nc.psum_tensor(C.uname("l3f_pst"), [1, 2048], F32))
                psb = esp.enter_context(nc.psum_tensor(C.uname("l3f_psb"), [128, 2 * 512], F32))
                pse = esp.enter_context(nc.psum_tensor(C.uname("l3f_pse"), [128, 2 * 512], F32))
                with C.block() as blk:
                    base_gld = C.s_gld.n
                    base_mm = C.s_mm.n
                    base_m2 = C.s_m2.n
                    base_v = C.s_v.n

                    base_mm_ex[e] = base_mm

                    @blk.gpsimd
                    def _(gp):
                        if e + 1 < 8:
                            nb = (e + 1) % 2
                            if e >= 1:
                                gp.wait_ge(C.s_mm.h, base_mm_ex[e - 1] + 33)
                            C.load_w_kt(gp, esw2[:, nb * KT * 1024:(nb + 1) * KT * 1024],
                                        C.dram["l3f_we"][e + 1], KT)
                            C.s_gld.inc(gp.dma_start(
                                ber2[0:1, nb * 1024:(nb + 1) * 1024],
                                C.dram["l3f_ber"][e + 1:e + 2, :]), 16)

                    def _gxpos(c):
                        return [2, 3, 12, 21][c]

                    def _evpos(c, ot):
                        return [4, 13, 22, 30][c] + ot

                    @blk.tensor
                    def _(t):
                        for tt in range(16):
                            ins = t.transpose(pst[0:1, tt * 128:(tt + 1) * 128],
                                              gcol[:, tt * 8 + e:tt * 8 + e + 1], C.identf[:])
                        C.s_mm.inc(ins, 1)
                        t.wait_ge(C.s_gld.h, base_gld0 + 32 * (e + 1))
                        t.wait_ge(C.s_v.h, base_v + 1)
                        for cc in range(2):
                            ins = t.matmul(psb[:, cc * 512:(cc + 1) * 512],
                                           C.ones_row[0:1, 0:128],
                                           growe[0:1, cc * 512:(cc + 1) * 512],
                                           start=True, stop=True)
                            C.s_m2.inc(ins, 1)
                        for c in range(4):
                            for ot in range(KT):
                                i2 = c * KT + ot
                                buf = i2 % 2
                                t.wait_ge(C.s_v.h, base_v + _gxpos(c))
                                if i2 >= 2:
                                    p2 = i2 - 2
                                    t.wait_ge(C.s_v.h,
                                              base_v + _evpos(p2 // KT, p2 % KT))
                                psl = pse[:, buf * 512:(buf + 1) * 512]
                                t.matmul(psl, ber[0:1, ot * 128:(ot + 1) * 128],
                                         growe[0:1, c * 512:(c + 1) * 512],
                                         start=True, stop=False)
                                for k in range(KT):
                                    ins = t.matmul(
                                        psl, we[:, k * 1024 + ot * 128: k * 1024 + (ot + 1) * 128],
                                        gx[:, ((c % 2) * KT + k) * 512:((c % 2) * KT + k + 1) * 512],
                                        start=False, stop=(k == KT - 1))
                                C.s_mm.inc(ins, 1)
                            if c + 2 < 4:
                                ins = t.matmul(psb[:, ((c + 2) % 2) * 512:((c + 2) % 2 + 1) * 512],
                                               C.ones_row[0:1, 0:128],
                                               growe[0:1, (c + 2) * 512:(c + 3) * 512],
                                               start=True, stop=True)
                                C.s_m2.inc(ins, 1)

                    @blk.vector
                    def _(v):
                        v.wait_ge(C.s_mm.h, base_mm + 1)
                        C.s_v.inc(v.tensor_copy(growe[0:1, :], pst[0:1, :]), 1)

                        def gx_build(c):
                            v.wait_ge(C.s_m2.h, base_m2 + c + 1)
                            if c >= 2:
                                v.wait_ge(C.s_mm.h, base_mm + 1 + KT * (c - 1))
                            for k in range(KT):
                                ins = v.tensor_mul(
                                    gx[:, ((c % 2) * KT + k) * 512:((c % 2) * KT + k + 1) * 512],
                                    C.xlntile(k, c * 512, (c + 1) * 512),
                                    psb[:, (c % 2) * 512:(c % 2 + 1) * 512])
                            C.s_v.inc(ins, 1)

                        gx_build(0)
                        gx_build(1)
                        for c in range(4):
                            for ot in range(KT):
                                i2 = c * KT + ot
                                buf = i2 % 2
                                v.wait_ge(C.s_mm.h, base_mm + 1 + i2 + 1)
                                dst = C.xtile(ot, c * 512, (c + 1) * 512)
                                C.s_v.inc(v.tensor_add(dst, dst,
                                                       pse[:, buf * 512:(buf + 1) * 512]), 1)
                            if c + 2 < 4:
                                gx_build(c + 2)


EMITTERS = {
    "l0_attn": emit_l0_attn,
    "l0_ffn": emit_l0_ffn,
    "l1_attn": emit_l1_attn,
    "l1_ffn": emit_l1_ffn,
    "l2_attn": emit_l2_attn,
    "l2_ffn": emit_l2_ffn,
    "l3_attn": emit_l3_attn,
    "l3_ffn": emit_l3_ffn,
}


def build_program(hp, blocks):
    nc = bass.Bass("TRN2", target_bir_lowering=False)
    es = contextlib.ExitStack()
    with es:
        C = KCtx(nc, es, hp)
        C.load_consts()
        emit_load_x(C)
        for b in blocks:
            EMITTERS[b](C)
        emit_store_x(C)
    return nc


def run_groups(inputs, groups):
    hp = host_prep(inputs)
    x = np.asarray(inputs["L0_attn_x"], np.float32)
    cur = [np.ascontiguousarray(x[b].T) for b in range(NCORES)]
    for blocks in groups:
        nc = build_program(hp, blocks)
        in_maps = [dict(hp, x=cur[b]) for b in range(NCORES)]
        res = run_bass_kernel_spmd(nc, in_maps, core_ids=list(range(NCORES)))
        cur = [np.asarray(res.results[b]["out"]) for b in range(NCORES)]
    out = np.stack([c.T for c in cur], axis=0)
    return np.ascontiguousarray(out)


def kernel(**inputs):
    groups = [BLOCK_NAMES]
    return run_groups(inputs, groups)
